# version 20
# baseline (speedup 1.0000x reference)
from collections import defaultdict
from contextlib import ExitStack

import numpy as np
import ml_dtypes

import concourse.bass as bass
import concourse.mybir as mybir
from concourse.bass_types import AP
from concourse.bass_utils import run_bass_kernel_spmd

F32 = mybir.dt.float32
BF16 = mybir.dt.bfloat16
FP8 = mybir.dt.float8e4
AF = mybir.ActivationFunctionType
AX = mybir.AxisListType
ALU = mybir.AluOpType

bfloat16 = ml_dtypes.bfloat16

B, N, D, H, DH = 16, 2048, 1024, 16, 64
NCORES, SPC = 8, 2
NT = N // 128
NQT = N // 512
ROPE_BASE = 10000.0

NEW2OLD = [1, 3, 5, 7, 9, 11, 13, 15, 0, 2, 4, 6, 8, 10, 12, 14]
PI_NEW = [1 if (NEW2OLD[h] >> 1) & 1 else 0 for h in range(H)]

QKT_ELEMS = 32768
C_X1 = 0
C_XR = 8192
C_OUT = 10240
C_XN2 = 12288
C_SQ = 13312
C_Q8 = 14336
C_WO = 16384
A_AOT = 16384
A_YT = 20480

VT_CHUNK = 1544
VT_B = 520
VT_ROW = 16 * VT_CHUNK


def _host_layout(w_qkv, w_o, log_xi):
    half = np.concatenate([np.arange(0, 64, 2), np.arange(1, 64, 2)])
    qperm, kperm, vperm = [], [], []
    for nh in range(H):
        oh = NEW2OLD[nh]
        base = oh * 64
        pp = half if PI_NEW[nh] else np.arange(64)
        qperm.append(base + pp)
        kperm.append(base + pp)
        vperm.append(base + np.arange(64))
    qperm = np.concatenate(qperm)
    kperm = np.concatenate(kperm)
    vperm = np.concatenate(vperm)
    colperm = np.concatenate([qperm, 1024 + kperm, 2048 + vperm])
    w_qkv_p = np.ascontiguousarray(np.asarray(w_qkv)[:, colperm])
    w_o_p = np.ascontiguousarray(np.asarray(w_o)[vperm, :])
    log_xi_p = np.ascontiguousarray(np.asarray(log_xi)[NEW2OLD])
    return w_qkv_p, w_o_p, log_xi_p


def _consts():
    ident = np.eye(128, dtype=bfloat16)
    ident_f32 = np.eye(128, dtype=np.float32)
    p = np.arange(128)[:, None]
    nn = np.arange(128)[None, :]
    mtri = (nn >= p).astype(bfloat16)
    pos = np.arange(N, dtype=np.float32)
    inv_freq = ROPE_BASE ** (-(np.arange(0, DH, 2, dtype=np.float32) / DH))
    theta = pos[:, None] * inv_freq[None, :]
    cos8 = np.cos(theta).astype(np.float32)
    sin8 = np.sin(theta).astype(np.float32)
    iota = pos.reshape(1, N).astype(np.float32)
    oconst = np.zeros((128, 128), dtype=bfloat16)
    oconst[64, 0:64] = 1.0
    oconst[0, 64:128] = 1.0
    ones_col = np.ones((128, 1), dtype=bfloat16)
    selc = np.zeros((128, 4), dtype=np.float32)
    for j, pp in enumerate((0, 1, 64, 65)):
        selc[pp, j] = 1.0
    return dict(ident=ident, ident_f32=ident_f32, mtri=mtri, cos8=cos8,
                sin8=sin8, iota=iota, oconst=oconst, ones_col=ones_col, selc=selc)


def mkap(t, pbase, pcount, foff, pattern):
    row = t.shape[1]
    return AP(t, pbase * row + foff, [[row, pcount]] + [list(x) for x in pattern])


class Prog:
    ENGS = ("te", "act", "dve", "gp", "sp")

    def __init__(self):
        self.ops = {e: [] for e in self.ENGS}
        self.cnt = defaultdict(int)

    def emit(self, eng, fn, waits=(), inc=None, inc_by=1):
        if inc is None and eng in ("dve", "act"):
            inc = eng
        self.ops[eng].append((tuple(waits), fn, inc, inc_by))
        if inc is not None:
            self.cnt[inc] += inc_by
            return self.cnt[inc]
        return None

    def now(self, key):
        return self.cnt[key]


def build_program():
    nc = bass.Bass("TRN2", target_bir_lowering=False)
    C = _consts()

    x_d = nc.dram_tensor("x", [SPC, N, D], F32, kind="ExternalInput").ap()
    wqkv_d = nc.dram_tensor("wqkv", [D, 3 * D], F32, kind="ExternalInput").ap()
    wo_d = nc.dram_tensor("wo", [D, D], F32, kind="ExternalInput").ap()
    lxi_d = nc.dram_tensor("lxi", [1, H], F32, kind="ExternalInput").ap()
    ffk_d = nc.dram_tensor("ffk", [D, D], BF16, kind="ExternalInput").ap()
    sin_d = nc.dram_tensor("sin_s", [1, 1], F32, kind="ExternalInput").ap()
    sk_d = nc.dram_tensor("sk_s", [1, 1], F32, kind="ExternalInput").ap()
    fb_d = nc.dram_tensor("fbias", [1, D], BF16, kind="ExternalInput").ap()
    out_d = nc.dram_tensor("out", [SPC, N, D], F32, kind="ExternalOutput").ap()
    wscr_d = nc.dram_tensor("wscr", [D, 3 * D], BF16, kind="Internal").ap()

    ident_d = nc.inline_tensor(C["ident"], "c_ident").ap()
    mtri_d = nc.inline_tensor(C["mtri"], "c_mtri").ap()
    cos_d = nc.inline_tensor(C["cos8"], "c_cos8").ap()
    sin_tab_d = nc.inline_tensor(C["sin8"], "c_sin8").ap()
    iota_d = nc.inline_tensor(C["iota"], "c_iota").ap()
    oconst_d = nc.inline_tensor(C["oconst"], "c_oconst").ap()
    onesc_d = nc.inline_tensor(C["ones_col"], "c_onesc").ap()
    selc_d = nc.inline_tensor(C["selc"], "c_selc").ap()

    es = ExitStack()
    sb = lambda name, shape, dt: es.enter_context(nc.sbuf_tensor(name, shape, dt))
    pst = lambda name, shape, dt: es.enter_context(nc.psum_tensor(name, shape, dt))

    ARENA1 = sb("arena1", [128, 24576], BF16)
    QKT = sb("qkt", [128, QKT_ELEMS], BF16)
    VT = sb("vt", [128, VT_ROW], BF16)
    SCR0 = sb("scr0", [128, 6144], BF16)
    PHAT = sb("phat", [128, 3072], BF16)
    XCUR = sb("xcur", [128, 6144], BF16)
    COSB = sb("cosb", [128, 64], F32)
    SINB = sb("sinb", [128, 64], F32)
    RCC = sb("rcc", [128, 4616], BF16)
    BCS = sb("bcs", [128, 512], BF16)
    IDENT = sb("identsb", [128, 128], BF16)
    MTRI = sb("mtrisb", [128, 128], BF16)
    ONESC = sb("onescsb", [128, 1], BF16)
    INVX = sb("invx", [128, 8], F32)
    LXI = sb("lxisb", [128, 16], F32)
    SCOL = sb("scolsb", [128, 16], F32)
    SELC = sb("selcsb", [128, 4], F32)
    FBC = sb("fbcsb", [128, 8], BF16)
    ROWS = sb("rowssb", [128, 32], F32)
    FBCF = sb("fbcfsb", [128, 8], F32)
    MNEG = sb("mnegsb", [128, 256], BF16)
    WST = sb("wstsb", [128, 1024], F32)

    SA_QT = 0
    SA_XS = 0
    SA_XB = 3072
    SA_XT = 4096
    SA_XSQ = 5120

    PS_A = pst("ps_a", [128, 2048], F32)
    PS_B = pst("ps_b", [128, 1024], F32)
    PS_C = pst("ps_c", [128, 512], F32)
    PS_TP = pst("ps_tp", [128, 1024], BF16)

    sems = {}
    for s in ("spx", "spx2", "spw", "spw2", "spc", "sptx", "sptq", "spta", "spta2", "sptn", "sptn2", "spty", "gp", "gp2", "gpw", "te", "act", "dve"):
        sems[s] = es.enter_context(nc.semaphore("sem_" + s))

    prog = Prog()

    def emit(eng, fn, waits=(), inc=None, inc_by=1):
        return prog.emit(eng, fn, waits, inc, inc_by)

    def wnow(*keys):
        return [(k, prog.now(k)) for k in keys]

    def emitseq(eng, fns, waits=()):
        v = None
        for j, fn in enumerate(fns):
            v = emit(eng, fn, waits=waits if j == 0 else (), inc=eng)
        return v

    def dmac(out_ap, in_ap):
        def fn(e, o=out_ap, i=in_ap):
            return e.dma_start(out=o, in_=i)
        return emit("sp", fn, inc="spc", inc_by=16)

    dmac(IDENT[:], ident_d)
    dmac(MTRI[:], mtri_d)
    dmac(ONESC[:], onesc_d)
    dmac(SELC[:], selc_d)
    def _fbdma(e):
        with nc.allow_non_contiguous_dma("tiny 1KB bias load"):
            return e.dma_start(out=FBC[:], in_=fb_d.rearrange("o (c p) -> o c p", p=128)[0].transpose([1, 0]))
    emit("sp", _fbdma, inc="spc", inc_by=16)
    for pp in (0, 1, 64, 65):
        dmac(LXI[pp:pp + 1, :], lxi_d)
    dmac(SCOL[0:1, 0:1], sin_d)
    dmac(SCOL[0:1, 1:2], sk_d)
    v_sp0 = prog.now("spc")

    emit("dve", lambda e: e.memset(INVX[:], 0.0), inc="dve")
    emit("act", lambda e: e.activation(INVX[0:2, :], LXI[0:2, 0:8], AF.Exp, scale=-1.0),
         waits=[("spc", v_sp0)] + wnow("dve"), inc="act")
    emit("act", lambda e: e.activation(INVX[64:66, :], LXI[64:66, 0:8], AF.Exp, scale=-1.0),
         inc="act")

    v_d0 = emitseq("dve", [
        lambda e: e.reciprocal(SCOL[0:1, 2:3], SCOL[0:1, 0:1]),
        lambda e: e.reciprocal(SCOL[0:1, 3:4], SCOL[0:1, 1:2]),
        lambda e: e.tensor_tensor(SCOL[0:1, 4:5], SCOL[0:1, 0:1], SCOL[0:1, 1:2], ALU.mult),
        lambda e: e.memset(SCOL[:, 11:12], 1e-5),
        lambda e: e.memset(SCOL[0:1, 12:13], 1.0),
        lambda e: e.tensor_copy(FBCF[:], FBC[:]),
        lambda e: e.memset(WST[0:1, 0:128], 1.0),
    ], waits=[("spc", v_sp0)])

    def scbc(e):
        r = None
        for j, src in enumerate((4, 2, 3)):
            r = e.matmul(PS_C[:, j:j + 1], WST[0:1, 0:128], SCOL[0:1, src:src + 1],
                         start=True, stop=True)
        return r
    v_t0 = emit("te", scbc, waits=[("dve", v_d0)], inc="te")
    emit("dve", lambda e: e.tensor_copy(SCOL[:, 8:11], PS_C[:, 0:3]),
         waits=[("te", v_t0)], inc="dve")
    S_COL, RSIN_COL, RSK_COL = SCOL[:, 8:9], SCOL[:, 9:10], SCOL[:, 10:11]

    def phase_A(s):
        if s == 0:
            WH2 = [WST[:], XCUR[:, 0:2048].bitcast(F32)]
            wcast = {}
            for n in range(24):
                f, q = n // 3, n % 3
                lo = q * 1024
                half = n % 2
                qeng, qsem = ("sp", "spw") if half == 0 else ("gp", "spw2")
                wldw = wnow("dve", "te", "gp", "gp2") if n < 2 else [("dve", wcast[n - 2])]
                def ld(e, f=f, lo=lo, half=half):
                    return e.dma_start(out=WH2[half], in_=wqkv_d[f * 128:(f + 1) * 128, lo:lo + 1024])
                vw = emit(qeng, ld, waits=wldw, inc=qsem, inc_by=16)
                dst = ARENA1[:, f * 3072 + lo: f * 3072 + lo + 1024]
                if q == 0:
                    def cst(e, dst=dst, half=half):
                        return e.tensor_scalar_mul(dst, WH2[half], 0.125)
                else:
                    def cst(e, dst=dst, half=half):
                        return e.tensor_copy(dst, WH2[half])
                wcast[n] = emit("dve", cst, waits=[(qsem, vw)] + (wnow("te") if n < 2 else []), inc="dve")
            for n in range(24):
                f, q = n // 3, n % 3
                lo = q * 1024
                dst = ARENA1[:, f * 3072 + lo: f * 3072 + lo + 1024]
                emit("gp", lambda e, f=f, lo=lo, dst=dst: e.dma_start(
                    out=wscr_d[f * 128:(f + 1) * 128, lo:lo + 1024], in_=dst),
                    waits=[("dve", wcast[n])], inc="gpw", inc_by=16)
        else:
            v_st = prog.now("gpw")
            for f in range(8):
                def ld2(e, f=f):
                    return e.dma_start(out=ARENA1[:, f * 3072:(f + 1) * 3072],
                                       in_=wscr_d[f * 128:(f + 1) * 128, :])
                emit("sp", ld2, waits=wnow("dve", "te", "gp", "gp2") + [("gpw", v_st)],
                     inc="spw", inc_by=16)
            emit("dve", lambda e: e.engine_nop(), waits=[("spw", prog.now("spw"))], inc="dve")
        v_wcast = prog.now("dve")

        if s == 0:
            for cg in range(6):
                def csmm(e, cg=cg):
                    r = None
                    for f in range(8):
                        e.ldweights(ONESC[:])
                        r = e.matmul(PS_C[0:1, 0:512], ONESC[:],
                                     ARENA1[:, f * 3072 + cg * 512: f * 3072 + (cg + 1) * 512],
                                     start=(f == 0), stop=(f == 7))
                    return r
                vt = emit("te", csmm, waits=[("dve", v_wcast), ("spw", prog.now("spw"))] + wnow("dve"), inc="te")
                emit("dve", lambda e, cg=cg: e.tensor_copy(RCC[0:1, cg * 512:(cg + 1) * 512], PS_C[0:1, 0:512]),
                     waits=[("te", vt)], inc="dve")
        v_colsum = prog.now("dve")

        QTOK = SCR0[:, 0:3072]
        XB = SCR0[:, 3072:4096]
        XSQT = SCR0[:, 5120:6144]
        XS = XCUR[:, 0:2048].bitcast(F32)
        XT = [SCR0[:, 4096:5120], XCUR[:, 2048:3072]]

        emitseq("dve", [
            lambda e: e.memset(mkap(VT, 0, 128, 64, [[VT_CHUNK, 16], [65, 8]]), 1.0),
            lambda e: e.memset(mkap(VT, 0, 128, VT_B, [[VT_CHUNK, 16], [128, 8]]), 1.0),
        ] + [
            (lambda g: lambda e: e.memset(mkap(VT, 0, 128, VT_B + 128 * g + 1,
                                               [[VT_CHUNK, 16], [1, 63]]), 0.0))(g)
            for g in range(8)
        ], waits=wnow("te", "act") + [("spta", prog.now("spta")), ("spta2", prog.now("spta2"))])

        st = {}

        def emit_stats_rows(k):
            mslot2 = (k % 2) * 128
            st[("strow", k)] = emitseq("dve", [
                lambda e, m=mslot2: e.tensor_scalar_mul(MNEG[0:1, m:m + 128], PS_C[0:1, 0:128], -1.0 / 1024.0),
                lambda e: e.tensor_scalar_mul(WST[0:1, 0:128], PS_C[0:1, 0:128], 1.0 / 1024.0),
                lambda e: e.tensor_scalar_mul(WST[0:1, 128:256], PS_C[0:1, 128:256], 1.0 / 1024.0),
                lambda e: e.tensor_tensor(WST[0:1, 256:384], WST[0:1, 0:128], WST[0:1, 0:128], ALU.mult),
                lambda e: e.tensor_tensor(WST[0:1, 128:256], WST[0:1, 128:256], WST[0:1, 256:384], ALU.subtract),
            ], waits=[("te", st[("stm", k)])])
            st[("rst", k)] = emitseq("act", [
                lambda e: e.activation(WST[0:1, 384:512], WST[0:1, 128:256], AF.Ln, bias=SCOL[0:1, 11:12]),
                lambda e: e.activation(WST[0:1, 384:512], WST[0:1, 384:512], AF.Exp, scale=-0.5),
            ], waits=[("dve", st[("strow", k)])])
            st[("rstdtr", k)] = emit("te", lambda e: e.transpose(PS_C[:, 256:257], WST[0:1, 384:512],
                                                                SCOL[0:1, 12:13]),
                                     waits=[("act", st[("rst", k)])], inc="te")

        def emit_stats_col(k):
            st[("rstdcol", k)] = emit("dve", lambda e, k=k: e.tensor_copy(ROWS[:, (k % 2):(k % 2) + 1],
                                                                         PS_C[:, 256:257]),
                                      waits=[("te", st[("rstdtr", k)])], inc="dve")

        def cs_load(kk, waits):
            emit("sp", lambda e, kk=kk: e.dma_start(out=COSB[:, (kk % 2) * 32:(kk % 2) * 32 + 32],
                                                    in_=cos_d[kk * 128:(kk + 1) * 128, :]),
                 waits=waits, inc="spc", inc_by=16)
            st[("cs", kk)] = emit("sp", lambda e, kk=kk: e.dma_start(
                out=SINB[:, (kk % 2) * 32:(kk % 2) * 32 + 32],
                in_=sin_tab_d[kk * 128:(kk + 1) * 128, :]), inc="spc", inc_by=16)

        cs_load(0, wnow("dve"))
        for k in range(NT + 1):
            if k < NT:
                w_prev_cast = [("act", st[("cast", k - 1)])] if k else wnow("dve")
                st[("xld", k)] = emit("sp", lambda e, k=k: e.dma_start(out=XS, in_=x_d[s, k * 128:(k + 1) * 128, :]),
                                      waits=w_prev_cast + wnow("gp", "gp2"), inc="spx", inc_by=16)
                w_xb = [("sptx", st[("xtr", k - 1)])] if k else wnow("te")
                st[("cast", k)] = emit("act", lambda e: e.activation(XB, XS, AF.Copy),
                                       waits=[("spx", st[("xld", k)])] + w_xb, inc="act")

                def xtrd(e, k=k):
                    return e.dma_start_transpose(XT[k % 2].rearrange("p (g c) -> p g c", g=8),
                                                 XB.rearrange("p (g c) -> p g c", g=8))
                wxtr = [("act", st[("cast", k)])]
                if k >= 2:
                    wxtr.append(("te", st[("r1KV", k - 2)]))
                st[("xtr", k)] = emit("sp", xtrd, waits=wxtr, inc="sptx", inc_by=16)
                st[("xsq", k)] = emit("dve", lambda e, k=k: e.tensor_tensor(XSQT, XT[k % 2], XT[k % 2], ALU.mult),
                                      waits=[("sptx", st[("xtr", k)])], inc="dve")

            if k >= 1:
                j = k - 1
                mslot = (j % 2) * 128
                wq = [("sptx", st[("xtr", j)]), ("dve", v_wcast)]
                if j >= 1:
                    wq.append(("act", st[("evQA", j - 1)]))

                def mkqkv(cgs):
                    def qkvh(e, j=j, cgs=cgs):
                        r = None
                        for f in range(8):
                            e.ldweights(XT[j % 2][:, f * 128:(f + 1) * 128])
                            for cg in cgs:
                                dstp = PS_A[:, cg * 512:(cg + 1) * 512] if cg < 4 else PS_B[:, (cg - 4) * 512:(cg - 3) * 512]
                                r = e.matmul(dstp, XT[j % 2][:, f * 128:(f + 1) * 128],
                                             ARENA1[:, f * 3072 + cg * 512: f * 3072 + (cg + 1) * 512],
                                             start=(f == 0), stop=False)
                        return r
                    return qkvh

                def mkrank1(cgs, mslot):
                    def r1(e, cgs=cgs, mslot=mslot):
                        e.ldweights(MNEG[0:1, mslot:mslot + 128])
                        r = None
                        for cg in cgs:
                            dstp = PS_A[:, cg * 512:(cg + 1) * 512] if cg < 4 else PS_B[:, (cg - 4) * 512:(cg - 3) * 512]
                            r = e.matmul(dstp, MNEG[0:1, mslot:mslot + 128],
                                         RCC[0:1, cg * 512:(cg + 1) * 512], start=False, stop=True)
                        return r
                    return r1

                emit("te", mkqkv((0, 1)), waits=wq, inc="te")
                st[("r1Q", j)] = emit("te", mkrank1((0, 1), mslot),
                                      waits=[("dve", st[("strow", j)]), ("dve", st[("rstdcol", j)]),
                                             ("dve", v_colsum)], inc="te")
                if k < NT:
                    def stm(e, k=k):
                        r = None
                        for f in range(8):
                            e.ldweights(ONESC[:])
                            e.matmul(PS_C[0:1, 0:128], ONESC[:], XT[k % 2][:, f * 128:(f + 1) * 128],
                                     start=(f == 0), stop=(f == 7))
                        for f in range(8):
                            e.ldweights(ONESC[:])
                            r = e.matmul(PS_C[0:1, 128:256], ONESC[:], XSQT[:, f * 128:(f + 1) * 128],
                                         start=(f == 0), stop=(f == 7))
                        return r
                    st[("stm", k)] = emit("te", stm,
                                          waits=[("dve", st[("xsq", k)]), ("sptx", st[("xtr", k)]),
                                                 ("dve", st[("rstdcol", k - 1)])],
                                          inc="te")
                    emit_stats_rows(k)

                wqkv = [("act", st[("evKA", j - 1)]), ("dve", st[("evKD", j - 1)])] if j >= 1 else []
                emit("te", mkqkv((2, 3, 4, 5)), waits=wqkv, inc="te")
                st[("r1KV", j)] = emit("te", mkrank1((2, 3, 4, 5), mslot), inc="te")
                if j >= 1:
                    emit("act", lambda e: e.nop(), waits=[("sptq", st[("qkev", j - 1)])], inc="act")
                wqe = [("te", st[("r1Q", j)])]
                if j >= 1:
                    wqe.append(("dve", st[("qlast", j - 1)]))
                st[("evQA", j)] = emitseq("act", [
                    (lambda cg, j=j: lambda e: e.activation(
                        QTOK[:, cg * 512:(cg + 1) * 512],
                        PS_A[:, cg * 512:(cg + 1) * 512],
                        AF.Copy, scale=ROWS[:, (j % 2):(j % 2) + 1]))(cg)
                    for cg in range(2)
                ], waits=wqe)
                st[("evKA", j)] = emit("act", (lambda j=j: lambda e: e.activation(
                    QTOK[:, 1024:1536], PS_A[:, 1024:1536],
                    AF.Copy, scale=ROWS[:, (j % 2):(j % 2) + 1]))(), waits=[("te", st[("r1KV", j)])], inc="act")
                wqd = [("te", st[("r1KV", j)])]
                if j >= 1:
                    wqd.append(("sptq", st[("qkev", j - 1)]))
                st[("evKD", j)] = emitseq("dve", [
                    (lambda cg, j=j: lambda e: e.tensor_scalar_mul(
                        QTOK[:, cg * 512:(cg + 1) * 512],
                        PS_A[:, cg * 512:(cg + 1) * 512] if cg < 4 else PS_B[:, (cg - 4) * 512:(cg - 3) * 512],
                        ROWS[:, (j % 2):(j % 2) + 1]))(cg)
                    for cg in range(3, 6)
                ], waits=wqd)
                rope_fns = []
                for part in range(2):
                    boff = part * 1024 + 64
                    t1 = mkap(SCR0, 0, 128, boff, [[128, 8], [1, 32]])
                    t2 = mkap(SCR0, 0, 128, boff + 32, [[128, 8], [1, 32]])
                    cs = mkap(COSB, 0, 128, (j % 2) * 32, [[0, 8], [1, 32]])
                    sn = mkap(SINB, 0, 128, (j % 2) * 32, [[0, 8], [1, 32]])
                    sc1 = mkap(XCUR, 0, 128, 3072, [[64, 8], [1, 32]])
                    sc2 = mkap(XCUR, 0, 128, 3104, [[64, 8], [1, 32]])
                    rope_fns += [
                        (lambda a, b, c: lambda e: e.tensor_tensor(a, b, c, ALU.mult))(sc1, t1, sn),
                        (lambda a, b, c: lambda e: e.tensor_tensor(a, b, c, ALU.mult))(sc2, t2, sn),
                        (lambda a, c: lambda e: e.tensor_tensor(a, a, c, ALU.mult))(t2, cs),
                        (lambda a, c: lambda e: e.tensor_tensor(a, a, c, ALU.mult))(t1, cs),
                        (lambda a, c: lambda e: e.tensor_tensor(a, a, c, ALU.subtract))(t1, sc2),
                        (lambda a, c: lambda e: e.tensor_tensor(a, a, c, ALU.add))(t2, sc1),
                    ]
                st[("rope", j)] = emitseq("dve", rope_fns,
                                          waits=[("spc", st[("cs", j)]), ("act", st[("evKA", j)]),
                                                 ("dve", st[("evKD", j)])])
                if j + 1 < NT:
                    cs_load(j + 1, [("dve", st[("rope", j)])])
                for part in range(2):
                    def qtrd(e, part=part, j=j):
                        dst = mkap(QKT, 0, 128, part * 16384 + j * 128, [[2048, 8], [1, 128]])
                        return e.dma_start_transpose(
                            dst, QTOK[:, part * 1024:(part + 1) * 1024].rearrange("p (g c) -> p g c", g=8))
                    wqt = [("dve", st[("rope", j)])] if part == 0 else ()
                    st[("qkev", j)] = emit("sp", qtrd, waits=wqt, inc="sptq", inc_by=16)
                cbv = j * VT_CHUNK
                st[("qlast", j)] = emitseq("dve", [
                    (lambda cb: lambda e: e.tensor_copy(
                        mkap(VT, 0, 128, cb, [[65, 8], [1, 64]]),
                        mkap(SCR0, 0, 128, 2048, [[128, 8], [1, 64]])))(cbv),
                    (lambda cb: lambda e: e.tensor_copy(
                        mkap(VT, 0, 128, cb + VT_B + 64, [[128, 8], [1, 64]]),
                        mkap(SCR0, 0, 128, 2048 + 64, [[128, 8], [1, 64]])))(cbv),
                ])
                if k < NT:
                    emit_stats_col(k)

            if k == 0:
                def stm0(e):
                    r = None
                    for f in range(8):
                        e.ldweights(ONESC[:])
                        e.matmul(PS_C[0:1, 0:128], ONESC[:], XT[0][:, f * 128:(f + 1) * 128],
                                 start=(f == 0), stop=(f == 7))
                    for f in range(8):
                        e.ldweights(ONESC[:])
                        r = e.matmul(PS_C[0:1, 128:256], ONESC[:], XSQT[:, f * 128:(f + 1) * 128],
                                     start=(f == 0), stop=(f == 7))
                    return r
                st[("stm", 0)] = emit("te", stm0,
                                      waits=[("dve", st[("xsq", 0)]), ("sptx", st[("xtr", 0)])], inc="te")
                emit_stats_rows(0)
                emit_stats_col(0)

    def phase_B(s):
        IOTAC = SCR0[:, 0:4096].bitcast(F32)
        for pp in (0, 1, 64, 65):
            emit("sp", lambda e, pp=pp: e.dma_start(out=IOTAC[pp:pp + 1, 0:2048], in_=iota_d),
                 waits=wnow("dve", "te") + [("sptq", prog.now("sptq")), ("sptx", prog.now("sptx"))],
                 inc="spc", inc_by=16)
        v_iota = prog.now("spc")
        emit("dve", lambda e: e.memset(PS_A[:], 0.0),
             waits=wnow("te", "act") + [("spc", v_iota), ("sptq", prog.now("sptq")), ("sptx", prog.now("sptx")),
                                        ("gpw", prog.now("gpw"))],
             inc="dve")

        v_gexp = [prog.now("act"), prog.now("act")]
        v_pvfree = prog.now("dve")
        v_bcprev = prog.now("act")
        v_xb = {}

        def xbuild_rows(g):
            hA, hB = 2 * g, 2 * g + 1
            kdst = XCUR[0:66, 0:2048] if g % 2 == 0 else SCR0[0:66, 4096:6144]
            emitseq("dve", [
                lambda e, hA=hA: e.tensor_tensor(ROWS[:, 12:13], INVX[:, hA:hA + 1], SELC[:, 0:1], ALU.mult),
                lambda e, hB=hB: e.tensor_tensor(ROWS[:, 13:14], INVX[:, hB:hB + 1], SELC[:, 2:3], ALU.mult),
                lambda e: e.tensor_tensor(ROWS[:, 12:13], ROWS[:, 12:13], ROWS[:, 13:14], ALU.add),
                lambda e, hA=hA: e.tensor_tensor(ROWS[:, 14:15], INVX[:, hA:hA + 1], SELC[:, 1:2], ALU.mult),
                lambda e, hB=hB: e.tensor_tensor(ROWS[:, 15:16], INVX[:, hB:hB + 1], SELC[:, 3:4], ALU.mult),
                lambda e: e.tensor_tensor(ROWS[:, 14:15], ROWS[:, 14:15], ROWS[:, 15:16], ALU.add),
                lambda e: e.tensor_scalar_mul(ROWS[:, 14:15], ROWS[:, 14:15], -1.0),
                lambda e: e.tensor_tensor(ROWS[:, 13:14], SELC[:, 1:2], SELC[:, 3:4], ALU.add),
                lambda e: e.tensor_tensor(ROWS[:, 15:16], SELC[:, 0:1], SELC[:, 2:3], ALU.add),
                lambda e, kdst=kdst: e.tensor_scalar(kdst, IOTAC[0:66, 0:2048],
                                                     ROWS[0:66, 12:13], ROWS[0:66, 13:14], ALU.mult, ALU.add),
            ], waits=wnow("te"))

        def xbuild_q(g):
            qo = 2048 + (g % 2) * 2048
            v_xb[g] = emit("dve", lambda e, qo=qo: e.tensor_scalar(
                XCUR[0:66, qo:qo + 2048], IOTAC[0:66, 0:2048],
                ROWS[0:66, 14:15], ROWS[0:66, 15:16], ALU.mult, ALU.add),
                waits=wnow("te"), inc="dve")

        xbuild_rows(0)
        xbuild_q(0)
        pending = []
        for g in range(8):
            epair = g < 4
            qo = 2048 + (g % 2) * 2048
            v_x = v_xb.get(g, prog.now("dve"))

            for i in range(NQT):
                CC = 4 * (i + 1)
                expv = {}
                mskv = {}

                def emit_S(c):
                    goff = (c % 2) * 1024
                    wS = [("dve", v_x)]
                    if c >= 2:
                        wS.append(("act", expv[c - 2]))
                    else:
                        wS.append(("act", v_gexp[c % 2]))
                    def smm(e, g=g, i=i, c=c, epair=epair, goff=goff, qo=qo):
                        r = None
                        nst = 128 * max(c - 4 * i, 0)
                        for hh in range(2):
                            off = goff + hh * 512
                            rb = hh * 64
                            lhs = QKT[rb:rb + 64,
                                      16384 + g * 2048 + c * 128: 16384 + g * 2048 + (c + 1) * 128]
                            rhs = QKT[rb:rb + 64,
                                      g * 2048 + i * 512 + nst: g * 2048 + (i + 1) * 512]
                            e.ldweights(lhs, tile_position=(rb, 0))
                            r = e.matmul(PS_A[:, off + nst: off + 512], lhs, rhs,
                                         start=True, stop=not epair, tile_position=(rb, 0))
                        if epair:
                            for hh in range(2):
                                off = goff + hh * 512
                                rb = hh * 64
                                if g % 2 == 0:
                                    lhs = XCUR[rb:rb + 2, c * 128:(c + 1) * 128]
                                else:
                                    lhs = SCR0[rb:rb + 2, 4096 + c * 128: 4096 + (c + 1) * 128]
                                rhs = XCUR[rb:rb + 2, qo + i * 512 + nst: qo + (i + 1) * 512]
                                e.ldweights(lhs, tile_position=(rb, 0))
                                r = e.matmul(PS_A[:, off + nst: off + 512], lhs, rhs,
                                             start=False, stop=True, tile_position=(rb, 0))
                        return r
                    vsm = emit("te", smm, waits=wS, inc="te")
                    nst0 = 128 * max(c - 4 * i, 0)
                    vex = emit("act", lambda e, c=c, goff=goff, nst0=nst0: e.activation(
                        PHAT[:, (c % 3) * 1024 + nst0:(c % 3) * 1024 + 1024],
                        PS_A[:, goff + nst0:goff + 1024], AF.Exp),
                        waits=[("te", vsm)], inc="act")
                    expv[c] = vex
                    dlt = c - 4 * i
                    if dlt >= 0:
                        pb = (c % 3) * 1024
                        emitseq("dve", [
                            (lambda off: lambda e: e.tensor_tensor(
                                PHAT[:, off:off + 128], PHAT[:, off:off + 128],
                                MTRI[:], ALU.mult))(pb + hh * 512 + 128 * dlt)
                            for hh in range(2)
                        ], waits=[("act", vex)])
                    mskv[c] = prog.now("dve")

                def emit_PV(c):
                    def pvf(e, g=g, i=i, c=c, CC=CC):
                        pb = (c % 3) * 1024
                        nst = 128 * max(c - 4 * i, 0)
                        cb = c * VT_CHUNK
                        r = None
                        for hh in range(2):
                            if hh == 0:
                                lhs = VT[:, cb + g * 65: cb + (g + 1) * 65]
                                dst = PS_B[0:65, nst:512]
                            else:
                                o = cb + VT_B + 128 * g
                                lhs = VT[:, o:o + 128]
                                dst = PS_B[:, 512 + nst:1024]
                            rhs = PHAT[:, pb + hh * 512 + nst: pb + (hh + 1) * 512]
                            e.ldweights(lhs)
                            r = e.matmul(dst, lhs, rhs, start=(c == 0), stop=(c == CC - 1))
                        return r
                    wpv = [("act", expv[c]), ("dve", mskv[c])]
                    if c == 0:
                        wpv.append(("dve", v_pvfree))
                    return emit("te", pvf, waits=wpv, inc="te")

                for c in range(CC):
                    emit_S(c)
                    if c >= 1 and pending:
                        pending.pop(0)()
                    if c >= 1:
                        v_pv = emit_PV(c - 1)
                v_pv = emit_PV(CC - 1)
                while pending:
                    pending.pop(0)()
                if epair and g + 1 < 4:
                    if i == 0:
                        xbuild_rows(g + 1)
                    elif i == 1:
                        xbuild_q(g + 1)

                v_gexp = [expv[CC - 2], expv[CC - 1]]
                o0 = g * 2048 + i * 512
                u = {"v_pv": v_pv}

                def st0(u=u, o0=o0):
                    nonlocal v_pvfree
                    v_pvfree = emitseq("dve", [
                        (lambda o0: lambda e: e.tensor_copy(ARENA1[0:64, o0:o0 + 512], PS_B[0:64, 0:512]))(o0),
                        (lambda o0: lambda e: e.tensor_copy(ARENA1[64:128, o0:o0 + 512], PS_B[64:128, 512:1024]))(o0),
                        lambda e: e.tensor_copy(RCC[64:65, 0:512], PS_B[64:65, 0:512]),
                        lambda e: e.tensor_copy(RCC[0:1, 3072:3584], PS_B[0:1, 512:1024]),
                    ], waits=[("te", u["v_pv"])])
                    u["v_pvfree"] = v_pvfree

                def st1(u=u):
                    def ltr(e):
                        r = None
                        for q4 in range(4):
                            r = e.transpose(PS_TP[:, 2 * q4:2 * q4 + 1], RCC[64:65, q4 * 128:(q4 + 1) * 128],
                                            ONESC[64:65, 0:1], tile_position=(64, 0))
                        for q4 in range(4):
                            r = e.transpose(PS_TP[:, 8 + 2 * q4:9 + 2 * q4],
                                            RCC[0:1, 3072 + q4 * 128:3072 + (q4 + 1) * 128],
                                            ONESC[0:1, 0:1])
                        return r
                    vltr = emit("te", ltr, waits=[("dve", u["v_pvfree"])], inc="te")
                    def vrecf(e):
                        with nc.allow_low_precision("bf16 softmax-normalizer broadcast"):
                            return e.reciprocal(RCC[:, 4608:4616], AP(PS_TP, 0, [[1024, 128], [2, 8]]))
                    u["vrec"] = emit("dve", vrecf, waits=[("te", vltr)], inc="dve")

                def st2(u=u):
                    def ltrb(e):
                        r = None
                        for q4 in range(8):
                            r = e.transpose(PS_C.bitcast(BF16)[0:1, q4 * 128:(q4 + 1) * 128],
                                            RCC[:, 4608 + q4:4609 + q4], IDENT[:])
                        return r
                    vltb = emit("te", ltrb, waits=[("dve", u["vrec"]), ("act", v_bcprev)], inc="te")
                    u["vrows"] = emitseq("dve", [
                        lambda e: e.tensor_copy(RCC[0:1, 3584:4096], PS_C.bitcast(BF16)[0:1, 0:512]),
                        lambda e: e.tensor_copy(RCC[0:1, 4096:4608], PS_C.bitcast(BF16)[0:1, 512:1024]),
                    ], waits=[("te", vltb)])

                def st3(u=u):
                    nonlocal v_bcprev
                    def bcmm(e):
                        e.ldweights(MTRI[0:1, 0:128])
                        e.matmul(PS_C[:, 0:512], MTRI[0:1, 0:128], RCC[0:1, 4096:4608],
                                 start=True, stop=True)
                        e.ldweights(MTRI[0:1, 64:128])
                        return e.matmul(PS_C[0:64, 0:512], MTRI[0:1, 64:128], RCC[0:1, 3584:4096],
                                        start=True, stop=True)
                    vbc = emit("te", bcmm, waits=[("dve", u["vrows"])], inc="te")
                    v_bcprev = emit("act", lambda e: e.activation(BCS[:], PS_C[:, 0:512], AF.Copy),
                                    waits=[("te", vbc)], inc="act")

                def st4(o0=o0):
                    emitseq("dve", [
                        (lambda o0: lambda e: e.tensor_tensor(ARENA1[0:64, o0:o0 + 512],
                                                              ARENA1[0:64, o0:o0 + 512], BCS[0:64, :], ALU.mult))(o0),
                        (lambda o0: lambda e: e.tensor_tensor(ARENA1[64:128, o0:o0 + 512],
                                                              ARENA1[64:128, o0:o0 + 512], BCS[64:128, :], ALU.mult))(o0),
                    ], waits=[("act", v_bcprev)])

                pending = [st0, st1, st2, st3, st4]
        while pending:
            pending.pop(0)()

    def phase_C(s):
        WOAR = QKT[:, C_WO:C_WO + 8192]
        FF8 = VT[:, 0:4096].bitcast(FP8)
        WH = [WST[:, 0:1024], XCUR[:, 0:2048].bitcast(F32)]
        wcv = {}
        for ic in range(8):
            wsem = "spw" if ic % 2 == 0 else "spc"
            wld = wnow("dve", "te", "act") if ic < 2 else [("dve", wcv[ic - 2])]
            def ldw(e, ic=ic):
                return e.dma_start(out=WH[ic % 2], in_=wo_d[ic * 128:(ic + 1) * 128, :])
            vw = emit("sp", ldw, waits=wld, inc=wsem, inc_by=16)
            wcv[ic] = emit("dve", lambda e, ic=ic: e.tensor_copy(
                WOAR[:, ic * 1024:(ic + 1) * 1024], WH[ic % 2]),
                waits=[(wsem, vw)] + (wnow("te") if ic < 2 else []), inc="dve")
        FKS = VT[:, 10240:12288]
        fcv = {}
        for ic in range(8):
            half = (ic % 2) * 1024
            fsem = "sptx" if ic % 2 == 0 else "sptq"
            fld = wnow("act", "dve", "te") if ic < 2 else [("act", fcv[ic - 2])]
            def ldf(e, ic=ic, half=half):
                return e.dma_start(out=FKS[:, half:half + 1024], in_=ffk_d[ic * 128:(ic + 1) * 128, :])
            vw = emit("sp", ldf, waits=fld, inc=fsem, inc_by=16)
            fcv[ic] = emit("act", lambda e, ic=ic, half=half: e.activation(
                FF8[:, ic * 1024:(ic + 1) * 1024], FKS[:, half:half + 1024],
                AF.Copy, scale=RSK_COL),
                waits=[(fsem, vw)] + (wnow("te") if ic < 2 else []), inc="act")
        v_wc = prog.now("act")

        X1b = [QKT[:, C_X1:C_X1 + 8192].bitcast(F32), VT[:, 16384:24576].bitcast(F32)]
        XR = [QKT[:, C_XR:C_XR + 2048].bitcast(F32), QKT[:, 24576:26624].bitcast(F32)]
        OUTT = [QKT[:, C_OUT:C_OUT + 2048].bitcast(F32), QKT[:, 26624:28672].bitcast(F32)]
        XN2 = [QKT[:, C_XN2:C_XN2 + 1024], QKT[:, 28672:29696]]
        SQb = [QKT[:, C_SQ:C_SQ + 1024], SCR0[:, 0:1024]]
        Q8Tb = [QKT[:, C_Q8:C_Q8 + 2048].bitcast(FP8), SCR0[:, 1024:3072].bitcast(FP8)]
        ATOK = [QKT[:, 29696:30720], QKT[:, 30720:31744]]
        ATOKA = [VT[:, 8192:9216], VT[:, 9216:10240]]
        XN2T = [QKT[:, 31744:32768], SCR0[:, 3072:4096]]
        AOT = [ARENA1[:, A_AOT:A_AOT + 4096], VT[:, 4096:8192], VT[:, 12288:16384]]

        ct = {}
        ffnv = {}
        for tg in range(NQT + 2):
            if tg < NQT:
                for oc in range(8):
                    bank = oc % 4
                    def opmm(e, oc=oc, bank=bank, tg=tg):
                        r = None
                        for g in range(8):
                            lhs = WOAR[:, g * 1024 + oc * 128: g * 1024 + (oc + 1) * 128]
                            e.ldweights(lhs)
                            r = e.matmul(PS_A[:, bank * 512:(bank + 1) * 512], lhs,
                                         ARENA1[:, g * 2048 + tg * 512: g * 2048 + (tg + 1) * 512],
                                         start=(g == 0), stop=(g == 7))
                        return r
                    if oc >= 4:
                        wop = [("act", ct[("aev", tg, oc - 4)])]
                    elif tg >= 1:
                        wop = [("act", ct[("aev", tg - 1, oc + 4)])]
                    else:
                        wop = wnow("dve") if oc == 0 else []
                    vo = emit("te", opmm, waits=wop, inc="te")
                    def aev(e, oc=oc, bank=bank, tg=tg):
                        t_ = AOT[tg % 3]
                        dst = AP(t_.tensor, t_.offset + oc * 128, [[t_.tensor.shape[1], 128], [1024, 4], [1, 128]])
                        return e.activation(dst, PS_A[:, bank * 512:(bank + 1) * 512]
                                            .rearrange("p (t c) -> p t c", c=128), AF.Copy)
                    wae = [("te", vo)]
                    if oc == 0 and tg >= 3:
                        wae += [("spta", ct[("aotr", tg - 3, 2)]), ("spta2", ct[("aotr", tg - 3, 3)])]
                    ct[("aev", tg, oc)] = emit("act", aev, waits=wae, inc="act")
                ct[("ao", tg)] = prog.now("act")

            if 1 <= tg <= NQT:
                tj = tg - 1
                t0 = tj * 512
                X1 = X1b[tj % 2]
                Q8T = Q8Tb[tj % 2]
                for tt in range(4):
                    ta = t0 + tt * 128
                    wx = wnow("dve") if (tj == 0 and tt < 2) else [("dve", ct[("x1", tj, tt - 2) if tt >= 2 else ("x1", tj - 1, tt + 2)])]
                    ct[("xld", tj, tt)] = emit("sp", lambda e, ta=ta, tt=tt: e.dma_start(
                        out=XR[tt % 2], in_=x_d[s, ta:ta + 128, :]),
                        waits=wx, inc="spx" if tt % 2 == 0 else "spx2", inc_by=16)
                    wao = [("act", ct[("ao", tj)])]
                    if tt >= 2:
                        wao.append(("dve", ct[("x1", tj, tt - 2)]))
                    elif tj >= 1:
                        wao.append(("dve", ct[("x1", tj - 1, tt + 2)]))
                    else:
                        wao += wnow("dve")
                    def aotrd(e, tj=tj, tt=tt):
                        t_ = AOT[tj % 3]
                        src_ap = AP(t_.tensor, t_.offset + tt * 1024, [[t_.tensor.shape[1], 128], [128, 8], [1, 128]])
                        return e.dma_start_transpose(
                            ATOKA[tt % 2].rearrange("p (g c) -> p g c", g=8), src_ap)
                    ct[("aotr", tj, tt)] = emit("sp", aotrd, waits=wao,
                                                inc="spta" if tt % 2 == 0 else "spta2", inc_by=16)
                    ct[("x1", tj, tt)] = emit("dve", lambda e, tt=tt, X1=X1: e.tensor_tensor(
                        X1[:, tt * 1024:(tt + 1) * 1024], ATOKA[tt % 2], XR[tt % 2], ALU.add),
                        waits=[("spta" if tt % 2 == 0 else "spta2", ct[("aotr", tj, tt)]),
                               ("spx" if tt % 2 == 0 else "spx2", ct[("xld", tj, tt)])], inc="dve")
                for tt in range(4):
                    x1v = X1[:, tt * 1024:(tt + 1) * 1024]
                    rb = 16 + 4 * tt
                    SQ = SQb[tt % 2]
                    wsq = [("dve", ct[("x1", tj, tt)])]
                    lnprev = (tt - 2) if tt >= 2 else (tt + 2)
                    if ("ln2", lnprev) in ct:
                        wsq.append(("dve", ct[("ln2", lnprev)]))
                    vsq = emit("act", lambda e, x1v=x1v, SQ=SQ: e.activation(SQ, x1v, AF.Square),
                               waits=wsq, inc="act")
                    wln = [("dve", ct[("x1", tj, tt)]), ("act", vsq)]
                    if ("rst", tt) in ct:
                        wln.append(("act", ct[("rst", tt)]))
                    ct[("ln2", tt)] = emitseq("dve", [
                        lambda e, x1v=x1v, rb=rb: e.reduce_sum(ROWS[:, rb:rb + 1], x1v, axis=AX.X),
                        lambda e, rb=rb, SQ=SQ: e.reduce_sum(ROWS[:, rb + 1:rb + 2], SQ, axis=AX.X),
                        lambda e, rb=rb: e.tensor_scalar_mul(ROWS[:, rb:rb + 1], ROWS[:, rb:rb + 1], 1.0 / 1024.0),
                        lambda e, rb=rb: e.tensor_scalar_mul(ROWS[:, rb + 1:rb + 2], ROWS[:, rb + 1:rb + 2], 1.0 / 1024.0),
                        lambda e, rb=rb: e.tensor_tensor(ROWS[:, rb + 2:rb + 3], ROWS[:, rb:rb + 1], ROWS[:, rb:rb + 1], ALU.mult),
                        lambda e, rb=rb: e.tensor_tensor(ROWS[:, rb + 1:rb + 2], ROWS[:, rb + 1:rb + 2], ROWS[:, rb + 2:rb + 3], ALU.subtract),
                    ], waits=wln)
                for tt in range(4):
                    rb = 16 + 4 * tt
                    ct[("rst", tt)] = emitseq("act", [
                        lambda e, rb=rb: e.activation(ROWS[:, rb + 2:rb + 3], ROWS[:, rb + 1:rb + 2], AF.Ln, bias=SCOL[:, 11:12]),
                        lambda e, rb=rb: e.activation(ROWS[:, rb + 2:rb + 3], ROWS[:, rb + 2:rb + 3], AF.Exp, scale=-0.5),
                    ], waits=[("dve", ct[("ln2", tt)])])
                for tt in range(4):
                    rb = 16 + 4 * tt
                    x1v = X1[:, tt * 1024:(tt + 1) * 1024]
                    wnrm = [("act", ct[("rst", tt)])]
                    if tt >= 2:
                        wnrm.append(("sptn" if tt % 2 == 0 else "sptn2", ct[("x2tr", tt - 2)]))
                    elif ("x2tr", tt + 2) in ct:
                        wnrm.append(("sptn" if tt % 2 == 0 else "sptn2", ct[("x2tr", tt + 2)]))
                    ct[("nrm", tt)] = emitseq("dve", [
                        lambda e, rb=rb: e.tensor_tensor(ROWS[:, rb + 3:rb + 4], ROWS[:, rb:rb + 1], ROWS[:, rb + 2:rb + 3], ALU.mult),
                        lambda e, rb=rb: e.tensor_scalar_mul(ROWS[:, rb + 3:rb + 4], ROWS[:, rb + 3:rb + 4], -1.0),
                        lambda e, x1v=x1v, rb=rb, tt=tt: e.tensor_scalar(XN2[tt % 2], x1v, ROWS[:, rb + 2:rb + 3],
                                                                        ROWS[:, rb + 3:rb + 4], ALU.mult, ALU.add),
                    ], waits=wnrm)
                    wx2 = [("dve", ct[("nrm", tt)])]
                    if tt >= 2:
                        wx2.append(("act", ct[("q8w", tt - 2)]))
                    elif ("q8w", tt + 2) in ct:
                        wx2.append(("act", ct[("q8w", tt + 2)]))
                    ct[("x2tr", tt)] = emit("sp", lambda e, tt=tt: e.dma_start_transpose(
                        XN2T[tt % 2].rearrange("p (g c) -> p g c", g=8),
                        XN2[tt % 2].rearrange("p (g c) -> p g c", g=8)),
                        waits=wx2, inc="sptn" if tt % 2 == 0 else "sptn2", inc_by=16)
                    def q8w2(e, tt=tt, Q8T=Q8T):
                        dst = Q8T.rearrange("p (g c) -> p g c", c=512)[:, :, tt * 128:(tt + 1) * 128]
                        return e.activation(dst, XN2T[tt % 2][:].rearrange("p (g c) -> p g c", g=8),
                                            AF.Copy, scale=RSIN_COL)
                    wq8 = [("sptn" if tt % 2 == 0 else "sptn2", ct[("x2tr", tt)]), ("act", v_wc)]
                    if (tj % 2) in ffnv:
                        wq8.append(("te", ffnv[tj % 2]))
                    ct[("q8w", tt)] = emit("act", q8w2, waits=wq8, inc="act")
                ct[("q8v", tj)] = prog.now("act")

            if tg >= 2:
                tb = tg - 2
                t0b = tb * 512
                X1o = X1b[tb % 2]
                Q8To = Q8Tb[tb % 2]
                yevv = {}
                for fc in range(8):
                    bank = fc % 2
                    def ffmm(e, fc=fc, bank=bank, Q8To=Q8To):
                        r = None
                        for ic in range(8):
                            lhs = FF8[:, ic * 1024 + fc * 128: ic * 1024 + (fc + 1) * 128]
                            e.ldweights(lhs)
                            r = e.matmul(PS_B[:, bank * 512:(bank + 1) * 512], lhs,
                                         Q8To[:, ic * 512:(ic + 1) * 512],
                                         start=(ic == 0), stop=(ic == 7))
                        return r
                    wff = [("act", yevv[fc - 2])] if fc >= 2 else [("act", ct[("q8v", tb)])]
                    if fc < 2 and ("ytr", 3) in ct:
                        wff.append(("spty", ct[("ytr", 3)]))
                    vf = emit("te", ffmm, waits=wff, inc="te")
                    def yev(e, fc=fc, bank=bank):
                        dst = AP(ARENA1, A_YT + fc * 128, [[24576, 128], [1024, 4], [1, 128]])
                        return e.activation(dst, PS_B[:, bank * 512:(bank + 1) * 512]
                                            .rearrange("p (t c) -> p t c", c=128), AF.Copy, scale=S_COL)
                    va2 = emit("act", yev, waits=[("te", vf)], inc="act")
                    yevv[fc] = va2
                    def ybias(e, fc=fc):
                        vap = AP(ARENA1, A_YT + fc * 128, [[24576, 128], [1024, 4], [1, 128]])
                        return e.tensor_scalar_add(vap, vap, FBCF[:, fc:fc + 1])
                    ct[("bias", fc)] = emit("dve", ybias, waits=[("act", va2)], inc="dve")
                    ffnv[tb % 2] = vf
                v_yb = prog.now("dve")

                for tt in range(4):
                    ta = t0b + tt * 128
                    wy = [("dve", v_yb)]
                    if (tb, tt) != (0, 0):
                        wy.append(("dve", ct[("out", tt - 1) if tt >= 1 else ("out", 3)]))
                    def ytrd(e, tt=tt):
                        src_ap = AP(ARENA1, A_YT + tt * 1024, [[24576, 128], [128, 8], [1, 128]])
                        return e.dma_start_transpose(
                            ATOK[tt % 2].rearrange("p (g c) -> p g c", g=8), src_ap)
                    ct[("ytr", tt)] = emit("sp", ytrd, waits=wy, inc="spty", inc_by=16)
                    gsem = "gp" if tt % 2 == 0 else "gp2"
                    wo_ = [("spty", ct[("ytr", tt)])]
                    if tt >= 2:
                        wo_.append((gsem, ct[("st", tt - 2)]))
                    else:
                        wo_ += wnow(gsem) if tb == 0 else [(gsem, ct[("st", tt + 2)])]
                    ct[("out", tt)] = emit("dve", lambda e, tt=tt, X1o=X1o: e.tensor_tensor(
                        OUTT[tt % 2], ATOK[tt % 2], X1o[:, tt * 1024:(tt + 1) * 1024], ALU.add),
                        waits=wo_, inc="dve")
                    ct[("st", tt)] = emit("gp", lambda e, ta=ta, tt=tt: e.dma_start(
                        out=out_d[s, ta:ta + 128, :], in_=OUTT[tt % 2]),
                        waits=[("dve", ct[("out", tt)])], inc=gsem, inc_by=16)



    for s in range(SPC):
        phase_A(s)
        phase_B(s)
        phase_C(s)

    emit("gp", lambda e: e.engine_nop(), waits=[("gp", prog.now("gp")), ("gp2", prog.now("gp2"))])

    with es:
        with nc.Block() as block:
            def runner(eng_key):
                selfsync = eng_key in ("dve", "act")
                def run(engine):
                    lw = defaultdict(int)
                    done = 0
                    for waits, fn, inc, inc_by in prog.ops[eng_key]:
                        allw = list(waits)
                        if selfsync and done > 0:
                            allw.append((eng_key, done))
                        for skey, val in allw:
                            if val > lw[skey]:
                                engine.wait_ge(sems[skey], val)
                                lw[skey] = val
                        instr = fn(engine)
                        if inc is not None:
                            instr.then_inc(sems[inc], inc_by)
                            if inc == eng_key:
                                done += inc_by
                return run
            block.tensor(runner("te"))
            block.scalar(runner("act"))
            block.vector(runner("dve"))
            block.gpsimd(runner("gp"))
            block.sync(runner("sp"))
    return nc


_NC_CACHE = {}


def _get_nc():
    if "nc" not in _NC_CACHE:
        _NC_CACHE["nc"] = build_program()
    return _NC_CACHE["nc"]


def kernel(L0_attn_x, L0_attn_w_qkv, L0_attn_w_o, L0_attn_log_xi,
           L0_ffn_kernel, L0_ffn_input_scale, L0_ffn_kernel_scale, L0_ffn_bias):
    x = np.ascontiguousarray(np.asarray(L0_attn_x), dtype=np.float32)
    w_qkv_p, w_o_p, log_xi_p = _host_layout(L0_attn_w_qkv, L0_attn_w_o, L0_attn_log_xi)
    ffk = np.ascontiguousarray(np.asarray(L0_ffn_kernel, dtype=bfloat16))
    fbias = np.ascontiguousarray(np.asarray(L0_ffn_bias, dtype=bfloat16)).reshape(1, D)
    s_in = np.asarray(L0_ffn_input_scale, dtype=np.float32).reshape(1, 1)
    s_k = np.asarray(L0_ffn_kernel_scale, dtype=np.float32).reshape(1, 1)
    lxi = log_xi_p.reshape(1, H).astype(np.float32)

    nc = _get_nc()
    in_maps = []
    for c in range(NCORES):
        in_maps.append(dict(
            x=np.ascontiguousarray(x[c * SPC:(c + 1) * SPC]),
            wqkv=w_qkv_p, wo=w_o_p, lxi=lxi, ffk=ffk,
            sin_s=s_in, sk_s=s_k, fbias=fbias,
        ))
    res = run_bass_kernel_spmd(nc, in_maps, core_ids=list(range(NCORES)))
    out = np.concatenate([r["out"] for r in res.results], axis=0)
    return out.reshape(B, N, D)


# version 21 (final)
# speedup vs baseline: 1.0668x; 1.0668x over previous
from collections import defaultdict
from contextlib import ExitStack

import numpy as np
import ml_dtypes

import concourse.bass as bass
import concourse.mybir as mybir
from concourse.bass_types import AP
from concourse.bass_utils import run_bass_kernel_spmd

F32 = mybir.dt.float32
BF16 = mybir.dt.bfloat16
FP8 = mybir.dt.float8e4
AF = mybir.ActivationFunctionType
AX = mybir.AxisListType
ALU = mybir.AluOpType

bfloat16 = ml_dtypes.bfloat16

B, N, D, H, DH = 16, 2048, 1024, 16, 64
NCORES, SPC = 8, 2
NT = N // 128
NQT = N // 512
ROPE_BASE = 10000.0

NEW2OLD = [1, 3, 5, 7, 9, 11, 13, 15, 0, 2, 4, 6, 8, 10, 12, 14]
PI_NEW = [1 if (NEW2OLD[h] >> 1) & 1 else 0 for h in range(H)]

QKT_ELEMS = 32768
C_X1 = 0
C_XR = 8192
C_OUT = 10240
C_XN2 = 12288
C_SQ = 13312
C_Q8 = 14336
C_WO = 16384
A_AOT = 16384
A_YT = 20480

VT_CHUNK = 1544
VT_B = 520
VT_ROW = 16 * VT_CHUNK


def _host_layout(w_qkv, w_o, log_xi):
    half = np.concatenate([np.arange(0, 64, 2), np.arange(1, 64, 2)])
    qperm, kperm, vperm = [], [], []
    for nh in range(H):
        oh = NEW2OLD[nh]
        base = oh * 64
        pp = half if PI_NEW[nh] else np.arange(64)
        qperm.append(base + pp)
        kperm.append(base + pp)
        vperm.append(base + np.arange(64))
    qperm = np.concatenate(qperm)
    kperm = np.concatenate(kperm)
    vperm = np.concatenate(vperm)
    colperm = np.concatenate([qperm, 1024 + kperm, 2048 + vperm])
    w_qkv_p = np.ascontiguousarray(np.asarray(w_qkv)[:, colperm])
    w_o_p = np.ascontiguousarray(np.asarray(w_o)[vperm, :])
    log_xi_p = np.ascontiguousarray(np.asarray(log_xi)[NEW2OLD])
    return w_qkv_p, w_o_p, log_xi_p


def _consts():
    ident = np.eye(128, dtype=bfloat16)
    ident_f32 = np.eye(128, dtype=np.float32)
    p = np.arange(128)[:, None]
    nn = np.arange(128)[None, :]
    mtri = (nn >= p).astype(bfloat16)
    pos = np.arange(N, dtype=np.float32)
    inv_freq = ROPE_BASE ** (-(np.arange(0, DH, 2, dtype=np.float32) / DH))
    theta = pos[:, None] * inv_freq[None, :]
    cos8 = np.cos(theta).astype(np.float32)
    sin8 = np.sin(theta).astype(np.float32)
    iota = pos.reshape(1, N).astype(np.float32)
    oconst = np.zeros((128, 128), dtype=bfloat16)
    oconst[64, 0:64] = 1.0
    oconst[0, 64:128] = 1.0
    ones_col = np.ones((128, 1), dtype=bfloat16)
    selc = np.zeros((128, 4), dtype=np.float32)
    for j, pp in enumerate((0, 1, 64, 65)):
        selc[pp, j] = 1.0
    return dict(ident=ident, ident_f32=ident_f32, mtri=mtri, cos8=cos8,
                sin8=sin8, iota=iota, oconst=oconst, ones_col=ones_col, selc=selc)


def mkap(t, pbase, pcount, foff, pattern):
    row = t.shape[1]
    return AP(t, pbase * row + foff, [[row, pcount]] + [list(x) for x in pattern])


class Prog:
    ENGS = ("te", "act", "dve", "gp", "sp")

    def __init__(self):
        self.ops = {e: [] for e in self.ENGS}
        self.cnt = defaultdict(int)

    def emit(self, eng, fn, waits=(), inc=None, inc_by=1):
        if inc is None and eng in ("dve", "act"):
            inc = eng
        self.ops[eng].append((tuple(waits), fn, inc, inc_by))
        if inc is not None:
            self.cnt[inc] += inc_by
            return self.cnt[inc]
        return None

    def now(self, key):
        return self.cnt[key]


def build_program():
    nc = bass.Bass("TRN2", target_bir_lowering=False)
    C = _consts()

    x_d = nc.dram_tensor("x", [SPC, N, D], F32, kind="ExternalInput").ap()
    wqkv_d = nc.dram_tensor("wqkv", [D, 3 * D], F32, kind="ExternalInput").ap()
    wo_d = nc.dram_tensor("wo", [D, D], F32, kind="ExternalInput").ap()
    lxi_d = nc.dram_tensor("lxi", [1, H], F32, kind="ExternalInput").ap()
    ffk_d = nc.dram_tensor("ffk", [D, D], BF16, kind="ExternalInput").ap()
    sin_d = nc.dram_tensor("sin_s", [1, 1], F32, kind="ExternalInput").ap()
    sk_d = nc.dram_tensor("sk_s", [1, 1], F32, kind="ExternalInput").ap()
    fb_d = nc.dram_tensor("fbias", [1, D], BF16, kind="ExternalInput").ap()
    out_d = nc.dram_tensor("out", [SPC, N, D], F32, kind="ExternalOutput").ap()
    wscr_d = nc.dram_tensor("wscr", [D, 3 * D], BF16, kind="Internal").ap()

    ident_d = nc.inline_tensor(C["ident"], "c_ident").ap()
    mtri_d = nc.inline_tensor(C["mtri"], "c_mtri").ap()
    cos_d = nc.inline_tensor(C["cos8"], "c_cos8").ap()
    sin_tab_d = nc.inline_tensor(C["sin8"], "c_sin8").ap()
    iota_d = nc.inline_tensor(C["iota"], "c_iota").ap()
    oconst_d = nc.inline_tensor(C["oconst"], "c_oconst").ap()
    onesc_d = nc.inline_tensor(C["ones_col"], "c_onesc").ap()
    selc_d = nc.inline_tensor(C["selc"], "c_selc").ap()

    es = ExitStack()
    sb = lambda name, shape, dt: es.enter_context(nc.sbuf_tensor(name, shape, dt))
    pst = lambda name, shape, dt: es.enter_context(nc.psum_tensor(name, shape, dt))

    ARENA1 = sb("arena1", [128, 24576], BF16)
    QKT = sb("qkt", [128, QKT_ELEMS], BF16)
    VT = sb("vt", [128, VT_ROW], BF16)
    SCR0 = sb("scr0", [128, 6144], BF16)
    PHAT = sb("phat", [128, 3072], BF16)
    XCUR = sb("xcur", [128, 6144], BF16)
    COSB = sb("cosb", [128, 64], F32)
    SINB = sb("sinb", [128, 64], F32)
    RCC = sb("rcc", [128, 4616], BF16)
    BCS = sb("bcs", [128, 512], BF16)
    IDENT = sb("identsb", [128, 128], BF16)
    MTRI = sb("mtrisb", [128, 128], BF16)
    ONESC = sb("onescsb", [128, 1], BF16)
    INVX = sb("invx", [128, 8], F32)
    LXI = sb("lxisb", [128, 16], F32)
    SCOL = sb("scolsb", [128, 16], F32)
    SELC = sb("selcsb", [128, 4], F32)
    FBC = sb("fbcsb", [128, 8], BF16)
    ROWS = sb("rowssb", [128, 32], F32)
    FBCF = sb("fbcfsb", [128, 8], F32)
    MNEG = sb("mnegsb", [128, 256], BF16)
    WST = sb("wstsb", [128, 1024], F32)

    SA_QT = 0
    SA_XS = 0
    SA_XB = 3072
    SA_XT = 4096
    SA_XSQ = 5120

    PS_A = pst("ps_a", [128, 2048], F32)
    PS_B = pst("ps_b", [128, 1024], F32)
    PS_C = pst("ps_c", [128, 512], F32)
    PS_TP = pst("ps_tp", [128, 1024], BF16)

    sems = {}
    for s in ("spx", "spx2", "spw", "spw2", "spc", "sptx", "sptq", "spta", "spta2", "sptn", "sptn2", "spty", "gp", "gp2", "gpw", "te", "act", "dve"):
        sems[s] = es.enter_context(nc.semaphore("sem_" + s))

    prog = Prog()

    def emit(eng, fn, waits=(), inc=None, inc_by=1):
        return prog.emit(eng, fn, waits, inc, inc_by)

    def wnow(*keys):
        return [(k, prog.now(k)) for k in keys]

    def emitseq(eng, fns, waits=()):
        v = None
        for j, fn in enumerate(fns):
            v = emit(eng, fn, waits=waits if j == 0 else (), inc=eng)
        return v

    def dmac(out_ap, in_ap):
        def fn(e, o=out_ap, i=in_ap):
            return e.dma_start(out=o, in_=i)
        return emit("sp", fn, inc="spc", inc_by=16)

    dmac(IDENT[:], ident_d)
    dmac(MTRI[:], mtri_d)
    dmac(ONESC[:], onesc_d)
    dmac(SELC[:], selc_d)
    def _fbdma(e):
        with nc.allow_non_contiguous_dma("tiny 1KB bias load"):
            return e.dma_start(out=FBC[:], in_=fb_d.rearrange("o (c p) -> o c p", p=128)[0].transpose([1, 0]))
    emit("sp", _fbdma, inc="spc", inc_by=16)
    for pp in (0, 1, 64, 65):
        dmac(LXI[pp:pp + 1, :], lxi_d)
    dmac(SCOL[0:1, 0:1], sin_d)
    dmac(SCOL[0:1, 1:2], sk_d)
    v_sp0 = prog.now("spc")

    emit("dve", lambda e: e.memset(INVX[:], 0.0), inc="dve")
    emit("act", lambda e: e.activation(INVX[0:2, :], LXI[0:2, 0:8], AF.Exp, scale=-1.0),
         waits=[("spc", v_sp0)] + wnow("dve"), inc="act")
    emit("act", lambda e: e.activation(INVX[64:66, :], LXI[64:66, 0:8], AF.Exp, scale=-1.0),
         inc="act")

    v_d0 = emitseq("dve", [
        lambda e: e.reciprocal(SCOL[0:1, 2:3], SCOL[0:1, 0:1]),
        lambda e: e.reciprocal(SCOL[0:1, 3:4], SCOL[0:1, 1:2]),
        lambda e: e.tensor_tensor(SCOL[0:1, 4:5], SCOL[0:1, 0:1], SCOL[0:1, 1:2], ALU.mult),
        lambda e: e.memset(SCOL[:, 11:12], 1e-5),
        lambda e: e.memset(SCOL[0:1, 12:13], 1.0),
        lambda e: e.tensor_copy(FBCF[:], FBC[:]),
        lambda e: e.memset(WST[0:1, 0:128], 1.0),
    ], waits=[("spc", v_sp0)])

    def scbc(e):
        r = None
        for j, src in enumerate((4, 2, 3)):
            r = e.matmul(PS_C[:, j:j + 1], WST[0:1, 0:128], SCOL[0:1, src:src + 1],
                         start=True, stop=True)
        return r
    v_t0 = emit("te", scbc, waits=[("dve", v_d0)], inc="te")
    emit("dve", lambda e: e.tensor_copy(SCOL[:, 8:11], PS_C[:, 0:3]),
         waits=[("te", v_t0)], inc="dve")
    S_COL, RSIN_COL, RSK_COL = SCOL[:, 8:9], SCOL[:, 9:10], SCOL[:, 10:11]

    def phase_A(s):
        if s == 0:
            WH2 = [WST[:], XCUR[:, 0:2048].bitcast(F32)]
            wcast = {}
            for n in range(24):
                f, q = n // 3, n % 3
                lo = q * 1024
                half = n % 2
                qeng, qsem = ("sp", "spw") if half == 0 else ("gp", "spw2")
                wldw = wnow("dve", "te", "gp", "gp2") if n < 2 else [("dve", wcast[n - 2])]
                def ld(e, f=f, lo=lo, half=half):
                    return e.dma_start(out=WH2[half], in_=wqkv_d[f * 128:(f + 1) * 128, lo:lo + 1024])
                vw = emit(qeng, ld, waits=wldw, inc=qsem, inc_by=16)
                dst = ARENA1[:, f * 3072 + lo: f * 3072 + lo + 1024]
                if q == 0:
                    def cst(e, dst=dst, half=half):
                        return e.tensor_scalar_mul(dst, WH2[half], 0.125)
                else:
                    def cst(e, dst=dst, half=half):
                        return e.tensor_copy(dst, WH2[half])
                wcast[n] = emit("dve", cst, waits=[(qsem, vw)] + (wnow("te") if n < 2 else []), inc="dve")
            for n in range(24):
                f, q = n // 3, n % 3
                lo = q * 1024
                dst = ARENA1[:, f * 3072 + lo: f * 3072 + lo + 1024]
                emit("gp", lambda e, f=f, lo=lo, dst=dst: e.dma_start(
                    out=wscr_d[f * 128:(f + 1) * 128, lo:lo + 1024], in_=dst),
                    waits=[("dve", wcast[n])], inc="gpw", inc_by=16)
        else:
            v_st = prog.now("gpw")
            for f in range(8):
                def ld2(e, f=f):
                    return e.dma_start(out=ARENA1[:, f * 3072:(f + 1) * 3072],
                                       in_=wscr_d[f * 128:(f + 1) * 128, :])
                emit("sp", ld2, waits=wnow("dve", "te", "gp", "gp2") + [("gpw", v_st)],
                     inc="spw", inc_by=16)
            emit("dve", lambda e: e.engine_nop(), waits=[("spw", prog.now("spw"))], inc="dve")
        v_wcast = prog.now("dve")

        if s == 0:
            for cg in range(6):
                def csmm(e, cg=cg):
                    r = None
                    for f in range(8):
                        e.ldweights(ONESC[:])
                        r = e.matmul(PS_C[0:1, 0:512], ONESC[:],
                                     ARENA1[:, f * 3072 + cg * 512: f * 3072 + (cg + 1) * 512],
                                     start=(f == 0), stop=(f == 7))
                    return r
                vt = emit("te", csmm, waits=[("dve", v_wcast), ("spw", prog.now("spw"))] + wnow("dve"), inc="te")
                emit("dve", lambda e, cg=cg: e.tensor_copy(RCC[0:1, cg * 512:(cg + 1) * 512], PS_C[0:1, 0:512]),
                     waits=[("te", vt)], inc="dve")
        v_colsum = prog.now("dve")

        QTOK = SCR0[:, 0:3072]
        XB = SCR0[:, 3072:4096]
        XSQT = SCR0[:, 5120:6144]
        XS = XCUR[:, 0:2048].bitcast(F32)
        XT = [SCR0[:, 4096:5120], XCUR[:, 2048:3072]]

        emitseq("dve", [
            lambda e: e.memset(mkap(VT, 0, 128, 64, [[VT_CHUNK, 16], [65, 8]]), 1.0),
            lambda e: e.memset(mkap(VT, 0, 128, VT_B, [[VT_CHUNK, 16], [128, 8]]), 1.0),
        ] + [
            (lambda g: lambda e: e.memset(mkap(VT, 0, 128, VT_B + 128 * g + 1,
                                               [[VT_CHUNK, 16], [1, 63]]), 0.0))(g)
            for g in range(8)
        ], waits=wnow("te", "act") + [("spta", prog.now("spta")), ("spta2", prog.now("spta2"))])

        st = {}

        def emit_stats_rows(k):
            mslot2 = (k % 2) * 128
            st[("strow", k)] = emitseq("dve", [
                lambda e, m=mslot2: e.tensor_scalar_mul(MNEG[0:1, m:m + 128], PS_C[0:1, 0:128], -1.0 / 1024.0),
                lambda e: e.tensor_scalar_mul(WST[0:1, 0:128], PS_C[0:1, 0:128], 1.0 / 1024.0),
                lambda e: e.tensor_scalar_mul(WST[0:1, 128:256], PS_C[0:1, 128:256], 1.0 / 1024.0),
                lambda e: e.tensor_tensor(WST[0:1, 256:384], WST[0:1, 0:128], WST[0:1, 0:128], ALU.mult),
                lambda e: e.tensor_tensor(WST[0:1, 128:256], WST[0:1, 128:256], WST[0:1, 256:384], ALU.subtract),
            ], waits=[("te", st[("stm", k)])])
            st[("rst", k)] = emitseq("act", [
                lambda e: e.activation(WST[0:1, 384:512], WST[0:1, 128:256], AF.Ln, bias=SCOL[0:1, 11:12]),
                lambda e: e.activation(WST[0:1, 384:512], WST[0:1, 384:512], AF.Exp, scale=-0.5),
            ], waits=[("dve", st[("strow", k)])])
            st[("rstdtr", k)] = emit("te", lambda e: e.transpose(PS_C[:, 256:257], WST[0:1, 384:512],
                                                                SCOL[0:1, 12:13]),
                                     waits=[("act", st[("rst", k)])], inc="te")

        def emit_stats_col(k):
            st[("rstdcol", k)] = emit("dve", lambda e, k=k: e.tensor_copy(ROWS[:, (k % 2):(k % 2) + 1],
                                                                         PS_C[:, 256:257]),
                                      waits=[("te", st[("rstdtr", k)])], inc="dve")

        def cs_load(kk, waits):
            emit("sp", lambda e, kk=kk: e.dma_start(out=COSB[:, (kk % 2) * 32:(kk % 2) * 32 + 32],
                                                    in_=cos_d[kk * 128:(kk + 1) * 128, :]),
                 waits=waits, inc="spc", inc_by=16)
            st[("cs", kk)] = emit("sp", lambda e, kk=kk: e.dma_start(
                out=SINB[:, (kk % 2) * 32:(kk % 2) * 32 + 32],
                in_=sin_tab_d[kk * 128:(kk + 1) * 128, :]), inc="spc", inc_by=16)

        cs_load(0, wnow("dve"))
        for k in range(NT + 1):
            if k < NT:
                w_prev_cast = [("act", st[("cast", k - 1)])] if k else wnow("dve")
                st[("xld", k)] = emit("sp", lambda e, k=k: e.dma_start(out=XS, in_=x_d[s, k * 128:(k + 1) * 128, :]),
                                      waits=w_prev_cast + wnow("gp", "gp2"), inc="spx", inc_by=16)
                w_xb = [("sptx", st[("xtr", k - 1)])] if k else wnow("te")
                st[("cast", k)] = emit("act", lambda e: e.activation(XB, XS, AF.Copy),
                                       waits=[("spx", st[("xld", k)])] + w_xb, inc="act")

                def xtrd(e, k=k):
                    return e.dma_start_transpose(XT[k % 2].rearrange("p (g c) -> p g c", g=8),
                                                 XB.rearrange("p (g c) -> p g c", g=8))
                wxtr = [("act", st[("cast", k)])]
                if k >= 2:
                    wxtr.append(("te", st[("qkv", k - 2)]))
                st[("xtr", k)] = emit("sp", xtrd, waits=wxtr, inc="sptx", inc_by=16)
                st[("xsq", k)] = emit("dve", lambda e, k=k: e.tensor_tensor(XSQT, XT[k % 2], XT[k % 2], ALU.mult),
                                      waits=[("sptx", st[("xtr", k)])], inc="dve")

            if k >= 1:
                j = k - 1
                mslot = (j % 2) * 128
                wq = [("sptx", st[("xtr", j)]), ("dve", v_wcast)]
                if j >= 1:
                    wq.append(("act", st[("qevac", j - 1)]))
                    wq.append(("dve", st[("qevacd", j - 1)]))

                def qkvmain(e, j=j):
                    r = None
                    for f in range(8):
                        e.ldweights(XT[j % 2][:, f * 128:(f + 1) * 128])
                        for cg in range(6):
                            dstp = PS_A[:, cg * 512:(cg + 1) * 512] if cg < 4 else PS_B[:, (cg - 4) * 512:(cg - 3) * 512]
                            r = e.matmul(dstp, XT[j % 2][:, f * 128:(f + 1) * 128],
                                         ARENA1[:, f * 3072 + cg * 512: f * 3072 + (cg + 1) * 512],
                                         start=(f == 0), stop=False)
                    return r
                emit("te", qkvmain, waits=wq, inc="te")
                if k < NT:
                    def stm(e, k=k):
                        r = None
                        for f in range(8):
                            e.ldweights(ONESC[:])
                            e.matmul(PS_C[0:1, 0:128], ONESC[:], XT[k % 2][:, f * 128:(f + 1) * 128],
                                     start=(f == 0), stop=(f == 7))
                        for f in range(8):
                            e.ldweights(ONESC[:])
                            r = e.matmul(PS_C[0:1, 128:256], ONESC[:], XSQT[:, f * 128:(f + 1) * 128],
                                         start=(f == 0), stop=(f == 7))
                        return r
                    st[("stm", k)] = emit("te", stm,
                                          waits=[("dve", st[("xsq", k)]), ("sptx", st[("xtr", k)]),
                                                 ("dve", st[("rstdcol", k - 1)])],
                                          inc="te")
                    emit_stats_rows(k)

                def rank1(e, j=j, mslot=mslot):
                    e.ldweights(MNEG[0:1, mslot:mslot + 128])
                    r = None
                    for cg in range(6):
                        dstp = PS_A[:, cg * 512:(cg + 1) * 512] if cg < 4 else PS_B[:, (cg - 4) * 512:(cg - 3) * 512]
                        r = e.matmul(dstp, MNEG[0:1, mslot:mslot + 128],
                                     RCC[0:1, cg * 512:(cg + 1) * 512], start=False, stop=True)
                    return r
                st[("qkv", j)] = emit("te", rank1, waits=[("dve", st[("strow", j)]), ("dve", st[("rstdcol", j)]), ("dve", v_colsum)], inc="te")
                if j >= 1:
                    emit("act", lambda e: e.nop(), waits=[("sptq", st[("qkev", j - 1)])], inc="act")
                wqe = [("te", st[("qkv", j)])]
                if j >= 1:
                    wqe.append(("dve", st[("qlast", j - 1)]))
                st[("qevac", j)] = emitseq("act", [
                    (lambda cg, j=j: lambda e: e.activation(
                        QTOK[:, cg * 512:(cg + 1) * 512],
                        PS_A[:, cg * 512:(cg + 1) * 512],
                        AF.Copy, scale=ROWS[:, (j % 2):(j % 2) + 1]))(cg)
                    for cg in range(3)
                ], waits=wqe)
                wqd = [("te", st[("qkv", j)])]
                if j >= 1:
                    wqd.append(("sptq", st[("qkev", j - 1)]))
                st[("qevacd", j)] = emitseq("dve", [
                    (lambda cg, j=j: lambda e: e.tensor_scalar_mul(
                        QTOK[:, cg * 512:(cg + 1) * 512],
                        PS_A[:, cg * 512:(cg + 1) * 512] if cg < 4 else PS_B[:, (cg - 4) * 512:(cg - 3) * 512],
                        ROWS[:, (j % 2):(j % 2) + 1]))(cg)
                    for cg in range(3, 6)
                ], waits=wqd)
                rope_fns = []
                for part in range(2):
                    boff = part * 1024 + 64
                    t1 = mkap(SCR0, 0, 128, boff, [[128, 8], [1, 32]])
                    t2 = mkap(SCR0, 0, 128, boff + 32, [[128, 8], [1, 32]])
                    cs = mkap(COSB, 0, 128, (j % 2) * 32, [[0, 8], [1, 32]])
                    sn = mkap(SINB, 0, 128, (j % 2) * 32, [[0, 8], [1, 32]])
                    sc1 = mkap(XCUR, 0, 128, 3072, [[64, 8], [1, 32]])
                    sc2 = mkap(XCUR, 0, 128, 3104, [[64, 8], [1, 32]])
                    rope_fns += [
                        (lambda a, b, c: lambda e: e.tensor_tensor(a, b, c, ALU.mult))(sc1, t1, sn),
                        (lambda a, b, c: lambda e: e.tensor_tensor(a, b, c, ALU.mult))(sc2, t2, sn),
                        (lambda a, c: lambda e: e.tensor_tensor(a, a, c, ALU.mult))(t2, cs),
                        (lambda a, c: lambda e: e.tensor_tensor(a, a, c, ALU.mult))(t1, cs),
                        (lambda a, c: lambda e: e.tensor_tensor(a, a, c, ALU.subtract))(t1, sc2),
                        (lambda a, c: lambda e: e.tensor_tensor(a, a, c, ALU.add))(t2, sc1),
                    ]
                st[("rope", j)] = emitseq("dve", rope_fns,
                                          waits=[("spc", st[("cs", j)]), ("act", st[("qevac", j)]),
                                                 ("dve", st[("qevacd", j)])])
                if j + 1 < NT:
                    cs_load(j + 1, [("dve", st[("rope", j)])])
                for part in range(2):
                    def qtrd(e, part=part, j=j):
                        dst = mkap(QKT, 0, 128, part * 16384 + j * 128, [[2048, 8], [1, 128]])
                        return e.dma_start_transpose(
                            dst, QTOK[:, part * 1024:(part + 1) * 1024].rearrange("p (g c) -> p g c", g=8))
                    wqt = [("dve", st[("rope", j)])] if part == 0 else ()
                    st[("qkev", j)] = emit("sp", qtrd, waits=wqt, inc="sptq", inc_by=16)
                cbv = j * VT_CHUNK
                st[("qlast", j)] = emitseq("dve", [
                    (lambda cb: lambda e: e.tensor_copy(
                        mkap(VT, 0, 128, cb, [[65, 8], [1, 64]]),
                        mkap(SCR0, 0, 128, 2048, [[128, 8], [1, 64]])))(cbv),
                    (lambda cb: lambda e: e.tensor_copy(
                        mkap(VT, 0, 128, cb + VT_B + 64, [[128, 8], [1, 64]]),
                        mkap(SCR0, 0, 128, 2048 + 64, [[128, 8], [1, 64]])))(cbv),
                ])
                if k < NT:
                    emit_stats_col(k)

            if k == 0:
                def stm0(e):
                    r = None
                    for f in range(8):
                        e.ldweights(ONESC[:])
                        e.matmul(PS_C[0:1, 0:128], ONESC[:], XT[0][:, f * 128:(f + 1) * 128],
                                 start=(f == 0), stop=(f == 7))
                    for f in range(8):
                        e.ldweights(ONESC[:])
                        r = e.matmul(PS_C[0:1, 128:256], ONESC[:], XSQT[:, f * 128:(f + 1) * 128],
                                     start=(f == 0), stop=(f == 7))
                    return r
                st[("stm", 0)] = emit("te", stm0,
                                      waits=[("dve", st[("xsq", 0)]), ("sptx", st[("xtr", 0)])], inc="te")
                emit_stats_rows(0)
                emit_stats_col(0)

    def phase_B(s):
        IOTAC = SCR0[:, 0:4096].bitcast(F32)
        for pp in (0, 1, 64, 65):
            emit("sp", lambda e, pp=pp: e.dma_start(out=IOTAC[pp:pp + 1, 0:2048], in_=iota_d),
                 waits=wnow("dve", "te") + [("sptq", prog.now("sptq")), ("sptx", prog.now("sptx"))],
                 inc="spc", inc_by=16)
        v_iota = prog.now("spc")
        emit("dve", lambda e: e.memset(PS_A[:], 0.0),
             waits=wnow("te", "act") + [("spc", v_iota), ("sptq", prog.now("sptq")), ("sptx", prog.now("sptx")),
                                        ("gpw", prog.now("gpw"))],
             inc="dve")

        v_gexp = [prog.now("act"), prog.now("act")]
        v_pvfree = prog.now("dve")
        v_bcprev = prog.now("act")
        v_xb = {}

        def xbuild_rows(g):
            hA, hB = 2 * g, 2 * g + 1
            kdst = XCUR[0:66, 0:2048] if g % 2 == 0 else SCR0[0:66, 4096:6144]
            emitseq("dve", [
                lambda e, hA=hA: e.tensor_tensor(ROWS[:, 12:13], INVX[:, hA:hA + 1], SELC[:, 0:1], ALU.mult),
                lambda e, hB=hB: e.tensor_tensor(ROWS[:, 13:14], INVX[:, hB:hB + 1], SELC[:, 2:3], ALU.mult),
                lambda e: e.tensor_tensor(ROWS[:, 12:13], ROWS[:, 12:13], ROWS[:, 13:14], ALU.add),
                lambda e, hA=hA: e.tensor_tensor(ROWS[:, 14:15], INVX[:, hA:hA + 1], SELC[:, 1:2], ALU.mult),
                lambda e, hB=hB: e.tensor_tensor(ROWS[:, 15:16], INVX[:, hB:hB + 1], SELC[:, 3:4], ALU.mult),
                lambda e: e.tensor_tensor(ROWS[:, 14:15], ROWS[:, 14:15], ROWS[:, 15:16], ALU.add),
                lambda e: e.tensor_scalar_mul(ROWS[:, 14:15], ROWS[:, 14:15], -1.0),
                lambda e: e.tensor_tensor(ROWS[:, 13:14], SELC[:, 1:2], SELC[:, 3:4], ALU.add),
                lambda e: e.tensor_tensor(ROWS[:, 15:16], SELC[:, 0:1], SELC[:, 2:3], ALU.add),
                lambda e, kdst=kdst: e.tensor_scalar(kdst, IOTAC[0:66, 0:2048],
                                                     ROWS[0:66, 12:13], ROWS[0:66, 13:14], ALU.mult, ALU.add),
            ], waits=wnow("te"))

        def xbuild_q(g):
            qo = 2048 + (g % 2) * 2048
            v_xb[g] = emit("dve", lambda e, qo=qo: e.tensor_scalar(
                XCUR[0:66, qo:qo + 2048], IOTAC[0:66, 0:2048],
                ROWS[0:66, 14:15], ROWS[0:66, 15:16], ALU.mult, ALU.add),
                waits=wnow("te"), inc="dve")

        xbuild_rows(0)
        xbuild_q(0)
        pending = []
        for g in range(8):
            epair = g < 4
            qo = 2048 + (g % 2) * 2048
            v_x = v_xb.get(g, prog.now("dve"))

            for i in range(NQT):
                CC = 4 * (i + 1)
                expv = {}
                mskv = {}

                def emit_S(c):
                    goff = (c % 2) * 1024
                    wS = [("dve", v_x)]
                    if c >= 2:
                        wS.append(("act", expv[c - 2]))
                    else:
                        wS.append(("act", v_gexp[c % 2]))
                    def smm(e, g=g, i=i, c=c, epair=epair, goff=goff, qo=qo):
                        r = None
                        nst = 128 * max(c - 4 * i, 0)
                        for hh in range(2):
                            off = goff + hh * 512
                            rb = hh * 64
                            lhs = QKT[rb:rb + 64,
                                      16384 + g * 2048 + c * 128: 16384 + g * 2048 + (c + 1) * 128]
                            rhs = QKT[rb:rb + 64,
                                      g * 2048 + i * 512 + nst: g * 2048 + (i + 1) * 512]
                            e.ldweights(lhs, tile_position=(rb, 0))
                            r = e.matmul(PS_A[:, off + nst: off + 512], lhs, rhs,
                                         start=True, stop=not epair, tile_position=(rb, 0))
                        if epair:
                            for hh in range(2):
                                off = goff + hh * 512
                                rb = hh * 64
                                if g % 2 == 0:
                                    lhs = XCUR[rb:rb + 2, c * 128:(c + 1) * 128]
                                else:
                                    lhs = SCR0[rb:rb + 2, 4096 + c * 128: 4096 + (c + 1) * 128]
                                rhs = XCUR[rb:rb + 2, qo + i * 512 + nst: qo + (i + 1) * 512]
                                e.ldweights(lhs, tile_position=(rb, 0))
                                r = e.matmul(PS_A[:, off + nst: off + 512], lhs, rhs,
                                             start=False, stop=True, tile_position=(rb, 0))
                        return r
                    vsm = emit("te", smm, waits=wS, inc="te")
                    nst0 = 128 * max(c - 4 * i, 0)
                    vex = emit("act", lambda e, c=c, goff=goff, nst0=nst0: e.activation(
                        PHAT[:, (c % 3) * 1024 + nst0:(c % 3) * 1024 + 1024],
                        PS_A[:, goff + nst0:goff + 1024], AF.Exp),
                        waits=[("te", vsm)], inc="act")
                    expv[c] = vex
                    dlt = c - 4 * i
                    if dlt >= 0:
                        pb = (c % 3) * 1024
                        emitseq("dve", [
                            (lambda off: lambda e: e.tensor_tensor(
                                PHAT[:, off:off + 128], PHAT[:, off:off + 128],
                                MTRI[:], ALU.mult))(pb + hh * 512 + 128 * dlt)
                            for hh in range(2)
                        ], waits=[("act", vex)])
                    mskv[c] = prog.now("dve")

                def emit_PV(c):
                    def pvf(e, g=g, i=i, c=c, CC=CC):
                        pb = (c % 3) * 1024
                        nst = 128 * max(c - 4 * i, 0)
                        cb = c * VT_CHUNK
                        r = None
                        for hh in range(2):
                            if hh == 0:
                                lhs = VT[:, cb + g * 65: cb + (g + 1) * 65]
                                dst = PS_B[0:65, nst:512]
                            else:
                                o = cb + VT_B + 128 * g
                                lhs = VT[:, o:o + 128]
                                dst = PS_B[:, 512 + nst:1024]
                            rhs = PHAT[:, pb + hh * 512 + nst: pb + (hh + 1) * 512]
                            e.ldweights(lhs)
                            r = e.matmul(dst, lhs, rhs, start=(c == 0), stop=(c == CC - 1))
                        return r
                    wpv = [("act", expv[c]), ("dve", mskv[c])]
                    if c == 0:
                        wpv.append(("dve", v_pvfree))
                    return emit("te", pvf, waits=wpv, inc="te")

                for c in range(CC):
                    emit_S(c)
                    if c >= 1 and pending:
                        pending.pop(0)()
                    if c >= 1:
                        v_pv = emit_PV(c - 1)
                v_pv = emit_PV(CC - 1)
                while pending:
                    pending.pop(0)()
                if epair and g + 1 < 4:
                    if i == 0:
                        xbuild_rows(g + 1)
                    elif i == 1:
                        xbuild_q(g + 1)

                v_gexp = [expv[CC - 2], expv[CC - 1]]
                o0 = g * 2048 + i * 512
                u = {"v_pv": v_pv}

                def st0(u=u, o0=o0):
                    nonlocal v_pvfree
                    v_pvfree = emitseq("dve", [
                        (lambda o0: lambda e: e.tensor_copy(ARENA1[0:64, o0:o0 + 512], PS_B[0:64, 0:512]))(o0),
                        (lambda o0: lambda e: e.tensor_copy(ARENA1[64:128, o0:o0 + 512], PS_B[64:128, 512:1024]))(o0),
                        lambda e: e.tensor_copy(RCC[64:65, 0:512], PS_B[64:65, 0:512]),
                        lambda e: e.tensor_copy(RCC[0:1, 3072:3584], PS_B[0:1, 512:1024]),
                    ], waits=[("te", u["v_pv"])])
                    u["v_pvfree"] = v_pvfree

                def st1(u=u):
                    def ltr(e):
                        r = None
                        for q4 in range(4):
                            r = e.transpose(PS_TP[:, 2 * q4:2 * q4 + 1], RCC[64:65, q4 * 128:(q4 + 1) * 128],
                                            ONESC[64:65, 0:1], tile_position=(64, 0))
                        for q4 in range(4):
                            r = e.transpose(PS_TP[:, 8 + 2 * q4:9 + 2 * q4],
                                            RCC[0:1, 3072 + q4 * 128:3072 + (q4 + 1) * 128],
                                            ONESC[0:1, 0:1])
                        return r
                    vltr = emit("te", ltr, waits=[("dve", u["v_pvfree"])], inc="te")
                    def vrecf(e):
                        with nc.allow_low_precision("bf16 softmax-normalizer broadcast"):
                            return e.reciprocal(RCC[:, 4608:4616], AP(PS_TP, 0, [[1024, 128], [2, 8]]))
                    u["vrec"] = emit("dve", vrecf, waits=[("te", vltr)], inc="dve")

                def st2(u=u):
                    def ltrb(e):
                        r = None
                        for q4 in range(8):
                            r = e.transpose(PS_C.bitcast(BF16)[0:1, q4 * 128:(q4 + 1) * 128],
                                            RCC[:, 4608 + q4:4609 + q4], IDENT[:])
                        return r
                    vltb = emit("te", ltrb, waits=[("dve", u["vrec"]), ("act", v_bcprev)], inc="te")
                    u["vrows"] = emitseq("dve", [
                        lambda e: e.tensor_copy(RCC[0:1, 3584:4096], PS_C.bitcast(BF16)[0:1, 0:512]),
                        lambda e: e.tensor_copy(RCC[0:1, 4096:4608], PS_C.bitcast(BF16)[0:1, 512:1024]),
                    ], waits=[("te", vltb)])

                def st3(u=u):
                    nonlocal v_bcprev
                    def bcmm(e):
                        e.ldweights(MTRI[0:1, 0:128])
                        e.matmul(PS_C[:, 0:512], MTRI[0:1, 0:128], RCC[0:1, 4096:4608],
                                 start=True, stop=True)
                        e.ldweights(MTRI[0:1, 64:128])
                        return e.matmul(PS_C[0:64, 0:512], MTRI[0:1, 64:128], RCC[0:1, 3584:4096],
                                        start=True, stop=True)
                    vbc = emit("te", bcmm, waits=[("dve", u["vrows"])], inc="te")
                    v_bcprev = emit("act", lambda e: e.activation(BCS[:], PS_C[:, 0:512], AF.Copy),
                                    waits=[("te", vbc)], inc="act")

                def st4(o0=o0):
                    emitseq("dve", [
                        (lambda o0: lambda e: e.tensor_tensor(ARENA1[0:64, o0:o0 + 512],
                                                              ARENA1[0:64, o0:o0 + 512], BCS[0:64, :], ALU.mult))(o0),
                        (lambda o0: lambda e: e.tensor_tensor(ARENA1[64:128, o0:o0 + 512],
                                                              ARENA1[64:128, o0:o0 + 512], BCS[64:128, :], ALU.mult))(o0),
                    ], waits=[("act", v_bcprev)])

                pending = [st0, st1, st2, st3, st4]
        while pending:
            pending.pop(0)()

    def phase_C(s):
        WOAR = QKT[:, C_WO:C_WO + 8192]
        FF8 = VT[:, 0:4096].bitcast(FP8)
        WH = [WST[:, 0:1024], XCUR[:, 0:2048].bitcast(F32)]
        wcv = {}
        for ic in range(8):
            wsem = "spw" if ic % 2 == 0 else "spc"
            wld = wnow("dve", "te", "act") if ic < 2 else [("dve", wcv[ic - 2])]
            def ldw(e, ic=ic):
                return e.dma_start(out=WH[ic % 2], in_=wo_d[ic * 128:(ic + 1) * 128, :])
            vw = emit("sp", ldw, waits=wld, inc=wsem, inc_by=16)
            wcv[ic] = emit("dve", lambda e, ic=ic: e.tensor_copy(
                WOAR[:, ic * 1024:(ic + 1) * 1024], WH[ic % 2]),
                waits=[(wsem, vw)] + (wnow("te") if ic < 2 else []), inc="dve")
        FKS = VT[:, 10240:12288]
        fcv = {}
        for ic in range(8):
            half = (ic % 2) * 1024
            fsem = "sptx" if ic % 2 == 0 else "sptq"
            fld = wnow("act", "dve", "te") if ic < 2 else [("act", fcv[ic - 2])]
            def ldf(e, ic=ic, half=half):
                return e.dma_start(out=FKS[:, half:half + 1024], in_=ffk_d[ic * 128:(ic + 1) * 128, :])
            vw = emit("sp", ldf, waits=fld, inc=fsem, inc_by=16)
            fcv[ic] = emit("act", lambda e, ic=ic, half=half: e.activation(
                FF8[:, ic * 1024:(ic + 1) * 1024], FKS[:, half:half + 1024],
                AF.Copy, scale=RSK_COL),
                waits=[(fsem, vw)] + (wnow("te") if ic < 2 else []), inc="act")
        v_wc = prog.now("act")

        X1b = [QKT[:, C_X1:C_X1 + 8192].bitcast(F32), VT[:, 16384:24576].bitcast(F32)]
        XR = [QKT[:, C_XR:C_XR + 2048].bitcast(F32), QKT[:, 24576:26624].bitcast(F32)]
        OUTT = [QKT[:, C_OUT:C_OUT + 2048].bitcast(F32), QKT[:, 26624:28672].bitcast(F32)]
        XN2 = [QKT[:, C_XN2:C_XN2 + 1024], QKT[:, 28672:29696]]
        SQb = [QKT[:, C_SQ:C_SQ + 1024], SCR0[:, 0:1024]]
        Q8Tb = [QKT[:, C_Q8:C_Q8 + 2048].bitcast(FP8), SCR0[:, 1024:3072].bitcast(FP8)]
        ATOK = [QKT[:, 29696:30720], QKT[:, 30720:31744]]
        ATOKA = [VT[:, 8192:9216], VT[:, 9216:10240]]
        XN2T = [QKT[:, 31744:32768], SCR0[:, 3072:4096]]
        AOT = [ARENA1[:, A_AOT:A_AOT + 4096], VT[:, 4096:8192], VT[:, 12288:16384]]

        ct = {}
        ffnv = {}
        for tg in range(NQT + 2):
            if tg < NQT:
                for oc in range(8):
                    bank = oc % 4
                    def opmm(e, oc=oc, bank=bank, tg=tg):
                        r = None
                        for g in range(8):
                            lhs = WOAR[:, g * 1024 + oc * 128: g * 1024 + (oc + 1) * 128]
                            e.ldweights(lhs)
                            r = e.matmul(PS_A[:, bank * 512:(bank + 1) * 512], lhs,
                                         ARENA1[:, g * 2048 + tg * 512: g * 2048 + (tg + 1) * 512],
                                         start=(g == 0), stop=(g == 7))
                        return r
                    if oc >= 4:
                        wop = [("act", ct[("aev", tg, oc - 4)])]
                    elif tg >= 1:
                        wop = [("act", ct[("aev", tg - 1, oc + 4)])]
                    else:
                        wop = wnow("dve") if oc == 0 else []
                    vo = emit("te", opmm, waits=wop, inc="te")
                    def aev(e, oc=oc, bank=bank, tg=tg):
                        t_ = AOT[tg % 3]
                        dst = AP(t_.tensor, t_.offset + oc * 128, [[t_.tensor.shape[1], 128], [1024, 4], [1, 128]])
                        return e.activation(dst, PS_A[:, bank * 512:(bank + 1) * 512]
                                            .rearrange("p (t c) -> p t c", c=128), AF.Copy)
                    wae = [("te", vo)]
                    if oc == 0 and tg >= 3:
                        wae += [("spta", ct[("aotr", tg - 3, 2)]), ("spta2", ct[("aotr", tg - 3, 3)])]
                    ct[("aev", tg, oc)] = emit("act", aev, waits=wae, inc="act")
                ct[("ao", tg)] = prog.now("act")

            if 1 <= tg <= NQT:
                tj = tg - 1
                t0 = tj * 512
                X1 = X1b[tj % 2]
                Q8T = Q8Tb[tj % 2]
                for tt in range(4):
                    ta = t0 + tt * 128
                    wx = wnow("dve") if (tj == 0 and tt < 2) else [("dve", ct[("x1", tj, tt - 2) if tt >= 2 else ("x1", tj - 1, tt + 2)])]
                    ct[("xld", tj, tt)] = emit("sp", lambda e, ta=ta, tt=tt: e.dma_start(
                        out=XR[tt % 2], in_=x_d[s, ta:ta + 128, :]),
                        waits=wx, inc="spx" if tt % 2 == 0 else "spx2", inc_by=16)
                    wao = [("act", ct[("ao", tj)])]
                    if tt >= 2:
                        wao.append(("dve", ct[("x1", tj, tt - 2)]))
                    elif tj >= 1:
                        wao.append(("dve", ct[("x1", tj - 1, tt + 2)]))
                    else:
                        wao += wnow("dve")
                    def aotrd(e, tj=tj, tt=tt):
                        t_ = AOT[tj % 3]
                        src_ap = AP(t_.tensor, t_.offset + tt * 1024, [[t_.tensor.shape[1], 128], [128, 8], [1, 128]])
                        return e.dma_start_transpose(
                            ATOKA[tt % 2].rearrange("p (g c) -> p g c", g=8), src_ap)
                    ct[("aotr", tj, tt)] = emit("sp", aotrd, waits=wao,
                                                inc="spta" if tt % 2 == 0 else "spta2", inc_by=16)
                    ct[("x1", tj, tt)] = emit("dve", lambda e, tt=tt, X1=X1: e.tensor_tensor(
                        X1[:, tt * 1024:(tt + 1) * 1024], ATOKA[tt % 2], XR[tt % 2], ALU.add),
                        waits=[("spta" if tt % 2 == 0 else "spta2", ct[("aotr", tj, tt)]),
                               ("spx" if tt % 2 == 0 else "spx2", ct[("xld", tj, tt)])], inc="dve")
                for tt in range(4):
                    x1v = X1[:, tt * 1024:(tt + 1) * 1024]
                    rb = 16 + 4 * tt
                    SQ = SQb[tt % 2]
                    wsq = [("dve", ct[("x1", tj, tt)])]
                    lnprev = (tt - 2) if tt >= 2 else (tt + 2)
                    if ("ln2", lnprev) in ct:
                        wsq.append(("dve", ct[("ln2", lnprev)]))
                    vsq = emit("act", lambda e, x1v=x1v, SQ=SQ: e.activation(SQ, x1v, AF.Square),
                               waits=wsq, inc="act")
                    wln = [("dve", ct[("x1", tj, tt)]), ("act", vsq)]
                    if ("rst", tt) in ct:
                        wln.append(("act", ct[("rst", tt)]))
                    ct[("ln2", tt)] = emitseq("dve", [
                        lambda e, x1v=x1v, rb=rb: e.reduce_sum(ROWS[:, rb:rb + 1], x1v, axis=AX.X),
                        lambda e, rb=rb, SQ=SQ: e.reduce_sum(ROWS[:, rb + 1:rb + 2], SQ, axis=AX.X),
                        lambda e, rb=rb: e.tensor_scalar_mul(ROWS[:, rb:rb + 1], ROWS[:, rb:rb + 1], 1.0 / 1024.0),
                        lambda e, rb=rb: e.tensor_scalar_mul(ROWS[:, rb + 1:rb + 2], ROWS[:, rb + 1:rb + 2], 1.0 / 1024.0),
                        lambda e, rb=rb: e.tensor_tensor(ROWS[:, rb + 2:rb + 3], ROWS[:, rb:rb + 1], ROWS[:, rb:rb + 1], ALU.mult),
                        lambda e, rb=rb: e.tensor_tensor(ROWS[:, rb + 1:rb + 2], ROWS[:, rb + 1:rb + 2], ROWS[:, rb + 2:rb + 3], ALU.subtract),
                    ], waits=wln)
                for tt in range(4):
                    rb = 16 + 4 * tt
                    ct[("rst", tt)] = emitseq("act", [
                        lambda e, rb=rb: e.activation(ROWS[:, rb + 2:rb + 3], ROWS[:, rb + 1:rb + 2], AF.Ln, bias=SCOL[:, 11:12]),
                        lambda e, rb=rb: e.activation(ROWS[:, rb + 2:rb + 3], ROWS[:, rb + 2:rb + 3], AF.Exp, scale=-0.5),
                    ], waits=[("dve", ct[("ln2", tt)])])
                for tt in range(4):
                    rb = 16 + 4 * tt
                    x1v = X1[:, tt * 1024:(tt + 1) * 1024]
                    wnrm = [("act", ct[("rst", tt)])]
                    if tt >= 2:
                        wnrm.append(("sptn" if tt % 2 == 0 else "sptn2", ct[("x2tr", tt - 2)]))
                    elif ("x2tr", tt + 2) in ct:
                        wnrm.append(("sptn" if tt % 2 == 0 else "sptn2", ct[("x2tr", tt + 2)]))
                    ct[("nrm", tt)] = emitseq("dve", [
                        lambda e, rb=rb: e.tensor_tensor(ROWS[:, rb + 3:rb + 4], ROWS[:, rb:rb + 1], ROWS[:, rb + 2:rb + 3], ALU.mult),
                        lambda e, rb=rb: e.tensor_scalar_mul(ROWS[:, rb + 3:rb + 4], ROWS[:, rb + 3:rb + 4], -1.0),
                        lambda e, x1v=x1v, rb=rb, tt=tt: e.tensor_scalar(XN2[tt % 2], x1v, ROWS[:, rb + 2:rb + 3],
                                                                        ROWS[:, rb + 3:rb + 4], ALU.mult, ALU.add),
                    ], waits=wnrm)
                    wx2 = [("dve", ct[("nrm", tt)])]
                    if tt >= 2:
                        wx2.append(("act", ct[("q8w", tt - 2)]))
                    elif ("q8w", tt + 2) in ct:
                        wx2.append(("act", ct[("q8w", tt + 2)]))
                    ct[("x2tr", tt)] = emit("sp", lambda e, tt=tt: e.dma_start_transpose(
                        XN2T[tt % 2].rearrange("p (g c) -> p g c", g=8),
                        XN2[tt % 2].rearrange("p (g c) -> p g c", g=8)),
                        waits=wx2, inc="sptn" if tt % 2 == 0 else "sptn2", inc_by=16)
                    def q8w2(e, tt=tt, Q8T=Q8T):
                        dst = Q8T.rearrange("p (g c) -> p g c", c=512)[:, :, tt * 128:(tt + 1) * 128]
                        return e.activation(dst, XN2T[tt % 2][:].rearrange("p (g c) -> p g c", g=8),
                                            AF.Copy, scale=RSIN_COL)
                    wq8 = [("sptn" if tt % 2 == 0 else "sptn2", ct[("x2tr", tt)]), ("act", v_wc)]
                    if (tj % 2) in ffnv:
                        wq8.append(("te", ffnv[tj % 2]))
                    ct[("q8w", tt)] = emit("act", q8w2, waits=wq8, inc="act")
                ct[("q8v", tj)] = prog.now("act")

            if tg >= 2:
                tb = tg - 2
                t0b = tb * 512
                X1o = X1b[tb % 2]
                Q8To = Q8Tb[tb % 2]
                yevv = {}
                FBANKS = [PS_B[:, 0:512], PS_B[:, 512:1024], PS_C[:, 0:512],
                          PS_TP.bitcast(F32)[:, 0:512]]
                for fc in range(8):
                    fb = FBANKS[fc % 4]
                    def ffmm(e, fc=fc, fb=fb, Q8To=Q8To):
                        r = None
                        for ic in range(8):
                            lhs = FF8[:, ic * 1024 + fc * 128: ic * 1024 + (fc + 1) * 128]
                            e.ldweights(lhs)
                            r = e.matmul(fb, lhs,
                                         Q8To[:, ic * 512:(ic + 1) * 512],
                                         start=(ic == 0), stop=(ic == 7))
                        return r
                    wff = [("act", yevv[fc - 4])] if fc >= 4 else [("act", ct[("q8v", tb)])]
                    if fc < 4 and ("ytr", 3) in ct:
                        wff.append(("spty", ct[("ytr", 3)]))
                    vf = emit("te", ffmm, waits=wff, inc="te")
                    def yev(e, fc=fc, fb=fb):
                        dst = AP(ARENA1, A_YT + fc * 128, [[24576, 128], [1024, 4], [1, 128]])
                        return e.activation(dst, fb
                                            .rearrange("p (t c) -> p t c", c=128), AF.Copy, scale=S_COL)
                    va2 = emit("act", yev, waits=[("te", vf)], inc="act")
                    yevv[fc] = va2
                    def ybias(e, fc=fc):
                        vap = AP(ARENA1, A_YT + fc * 128, [[24576, 128], [1024, 4], [1, 128]])
                        return e.tensor_scalar_add(vap, vap, FBCF[:, fc:fc + 1])
                    ct[("bias", fc)] = emit("dve", ybias, waits=[("act", va2)], inc="dve")
                    ffnv[tb % 2] = vf
                v_yb = prog.now("dve")

                for tt in range(4):
                    ta = t0b + tt * 128
                    wy = [("dve", v_yb)]
                    if (tb, tt) != (0, 0):
                        wy.append(("dve", ct[("out", tt - 1) if tt >= 1 else ("out", 3)]))
                    def ytrd(e, tt=tt):
                        src_ap = AP(ARENA1, A_YT + tt * 1024, [[24576, 128], [128, 8], [1, 128]])
                        return e.dma_start_transpose(
                            ATOK[tt % 2].rearrange("p (g c) -> p g c", g=8), src_ap)
                    ct[("ytr", tt)] = emit("sp", ytrd, waits=wy, inc="spty", inc_by=16)
                    gsem = "gp" if tt % 2 == 0 else "gp2"
                    wo_ = [("spty", ct[("ytr", tt)])]
                    if tt >= 2:
                        wo_.append((gsem, ct[("st", tt - 2)]))
                    else:
                        wo_ += wnow(gsem) if tb == 0 else [(gsem, ct[("st", tt + 2)])]
                    ct[("out", tt)] = emit("dve", lambda e, tt=tt, X1o=X1o: e.tensor_tensor(
                        OUTT[tt % 2], ATOK[tt % 2], X1o[:, tt * 1024:(tt + 1) * 1024], ALU.add),
                        waits=wo_, inc="dve")
                    ct[("st", tt)] = emit("gp", lambda e, ta=ta, tt=tt: e.dma_start(
                        out=out_d[s, ta:ta + 128, :], in_=OUTT[tt % 2]),
                        waits=[("dve", ct[("out", tt)])], inc=gsem, inc_by=16)



    for s in range(SPC):
        phase_A(s)
        phase_B(s)
        phase_C(s)

    emit("gp", lambda e: e.engine_nop(), waits=[("gp", prog.now("gp")), ("gp2", prog.now("gp2"))])

    with es:
        with nc.Block() as block:
            def runner(eng_key):
                selfsync = eng_key in ("dve", "act")
                def run(engine):
                    lw = defaultdict(int)
                    done = 0
                    for waits, fn, inc, inc_by in prog.ops[eng_key]:
                        allw = list(waits)
                        if selfsync and done > 0:
                            allw.append((eng_key, done))
                        for skey, val in allw:
                            if val > lw[skey]:
                                engine.wait_ge(sems[skey], val)
                                lw[skey] = val
                        instr = fn(engine)
                        if inc is not None:
                            instr.then_inc(sems[inc], inc_by)
                            if inc == eng_key:
                                done += inc_by
                return run
            block.tensor(runner("te"))
            block.scalar(runner("act"))
            block.vector(runner("dve"))
            block.gpsimd(runner("gp"))
            block.sync(runner("sp"))
    return nc


_NC_CACHE = {}


def _get_nc():
    if "nc" not in _NC_CACHE:
        _NC_CACHE["nc"] = build_program()
    return _NC_CACHE["nc"]


def kernel(L0_attn_x, L0_attn_w_qkv, L0_attn_w_o, L0_attn_log_xi,
           L0_ffn_kernel, L0_ffn_input_scale, L0_ffn_kernel_scale, L0_ffn_bias):
    x = np.ascontiguousarray(np.asarray(L0_attn_x), dtype=np.float32)
    w_qkv_p, w_o_p, log_xi_p = _host_layout(L0_attn_w_qkv, L0_attn_w_o, L0_attn_log_xi)
    ffk = np.ascontiguousarray(np.asarray(L0_ffn_kernel, dtype=bfloat16))
    fbias = np.ascontiguousarray(np.asarray(L0_ffn_bias, dtype=bfloat16)).reshape(1, D)
    s_in = np.asarray(L0_ffn_input_scale, dtype=np.float32).reshape(1, 1)
    s_k = np.asarray(L0_ffn_kernel_scale, dtype=np.float32).reshape(1, 1)
    lxi = log_xi_p.reshape(1, H).astype(np.float32)

    nc = _get_nc()
    in_maps = []
    for c in range(NCORES):
        in_maps.append(dict(
            x=np.ascontiguousarray(x[c * SPC:(c + 1) * SPC]),
            wqkv=w_qkv_p, wo=w_o_p, lxi=lxi, ffk=ffk,
            sin_s=s_in, sk_s=s_k, fbias=fbias,
        ))
    res = run_bass_kernel_spmd(nc, in_maps, core_ids=list(range(NCORES)))
    out = np.concatenate([r["out"] for r in res.results], axis=0)
    return out.reshape(B, N, D)
